# version 04
# baseline (speedup 1.0000x reference)
import numpy as np
import ml_dtypes
import concourse.bass as bass
import concourse.mybir as mybir
from concourse.bass_utils import run_bass_kernel_spmd

F32 = mybir.dt.float32
BF16 = mybir.dt.bfloat16
AF = mybir.ActivationFunctionType
ALU = mybir.AluOpType
AX = mybir.AxisListType.X

NCORE = 8
T = 2048
TH = 1024
D = 1024
DC = 8
NH = 16
DHD = 64
F_MOE = 1408
F_L1 = 2816
NEXP = 8
NSH = 2
BF = ml_dtypes.bfloat16

ENGINES = ("sync", "gpsimd", "vector", "scalar", "tensor")


class Sem:
    def __init__(self, nc, name):
        self.h = nc.alloc_semaphore(name)
        self.n = 0

    def bump(self, k=1):
        self.n += k
        return self.n


class Prog:
    def __init__(self, nc):
        self.nc = nc
        self.ops = {e: [] for e in ENGINES}
        self.sW = Sem(nc, "sW")
        self.sX = Sem(nc, "sX")
        self.sT = Sem(nc, "sT")
        self.sV = Sem(nc, "sV")
        self.sA = Sem(nc, "sA")
        self.sP = Sem(nc, "sP")

    def emit(self, eng, fn, waits=(), inc=None):
        waits = tuple((s.h, v) for (s, v) in waits if v > 0)
        inc_h = None
        if inc is not None:
            inc[0].bump(inc[1])
            inc_h = (inc[0].h, inc[1])

        def run(e, fn=fn, waits=waits, inc_h=inc_h):
            for h, v in waits:
                e.wait_ge(h, v)
            ins = fn(e)
            if inc_h is not None:
                ins.then_inc(inc_h[0], inc_h[1])
        self.ops[eng].append(run)

    def wait(self, eng, sem, val):
        if val > 0:
            self.emit(eng, lambda e: None, waits=[(sem, val)])

    def dma(self, queue, out, in_, waits=(), transpose=False):
        sem = self.sW if queue == "gpsimd" else self.sX
        self.emit(queue, lambda e, o=out, i=in_, t=transpose: e.dma_start(out=o, in_=i, transpose=t),
                  waits=waits, inc=(sem, 16))
        return sem.n

    def build(self):
        nc = self.nc
        with nc.Block() as block:
            @block.sync
            def _(e):
                for f in self.ops["sync"]:
                    f(e)

            @block.gpsimd
            def _(e):
                for f in self.ops["gpsimd"]:
                    f(e)

            @block.vector
            def _(e):
                for f in self.ops["vector"]:
                    f(e)

            @block.scalar
            def _(e):
                for f in self.ops["scalar"]:
                    f(e)

            @block.tensor
            def _(e):
                for f in self.ops["tensor"]:
                    f(e)
        return nc


class Ctx:
    def __init__(self, nc, p: Prog):
        self.nc = nc
        self.p = p
        self.xbig = nc.alloc_sbuf_tensor("xbig", [128, DC * T], F32)
        self.xnbig = nc.alloc_sbuf_tensor("xnbig", [128, DC * T], BF16)
        self.wbig = nc.alloc_sbuf_tensor("wbig", [128, 33792], BF16)
        self.hbig = nc.alloc_sbuf_tensor("hbig", [128, 11264], BF16)
        self.arena = nc.alloc_sbuf_tensor("arena", [128, 3072], F32)
        self.sm32 = nc.alloc_sbuf_tensor("sm32", [128, 512], F32)
        self.smbf = nc.alloc_sbuf_tensor("smbf", [128, 1024], BF16)
        self.onehot = nc.alloc_sbuf_tensor("onehot", [8, 1024], BF16)
        self.wte_em = nc.alloc_sbuf_tensor("wte_em", [8, 1024], BF16)
        self.ident_f32 = nc.alloc_sbuf_tensor("ident_f32", [128, 128], F32)
        self.pb = [nc.alloc_psum_tensor(f"pb{i}", [128, 512], F32) for i in range(8)]
        self.x_tok = [0] * DC
        self.guard = {}

    def x(self, dc):
        return self.xbig[:, dc * T:(dc + 1) * T]

    def xnh(self, th, dc):
        return self.xnbig[:, th * 8192 + dc * TH: th * 8192 + (dc + 1) * TH]

    def xnf(self, dc):
        return self.xnbig[:, dc * T:(dc + 1) * T]


def emit_ln(ctx, th, full=False):
    p = ctx.p
    m_bc = ctx.arena[:, 0:TH]
    r_bc = ctx.arena[:, TH:2 * TH]
    ones_bf = ctx.smbf[:, 0:128]

    def xc_slot(i):
        return ctx.hbig[:, (i % 4) * 512:(i % 4) * 512 + 512]

    def sq_slot(i):
        return ctx.hbig[:, 2048 + (i % 4) * 512:2048 + (i % 4) * 512 + 512]

    mm_tok = {}
    hb_guard = ctx.guard.get("hbig_free", 0)
    base = ctx.guard.get("consts_ready", 0)
    prev_evac = 0

    for tc in range(2):
        cs = slice(th * TH + tc * 512, th * TH + (tc + 1) * 512)
        hs = slice(tc * 512, (tc + 1) * 512)
        pm, pv = ctx.pb[0], ctx.pb[1]
        for dc in range(DC):
            i = tc * 8 + dc
            first = (i == 0)
            wv = [(p.sT, mm_tok[i - 4])] if i - 4 in mm_tok else []
            wa = list(wv)
            if first:
                extra = [(p.sV, base), (p.sT, hb_guard)] + \
                        [(p.sV, ctx.x_tok[dcx]) for dcx in range(DC)]
                wv += extra
                wa += extra
            p.emit("vector", lambda e, o=xc_slot(i), a=ctx.x(dc)[:, cs]:
                   e.tensor_copy(out=o, in_=a), waits=wv, inc=(p.sV, 1))
            c_tok = p.sV.n
            p.emit("scalar", lambda e, o=sq_slot(i), a=ctx.x(dc)[:, cs]:
                   e.activation(out=o, in_=a, func=AF.Square), waits=wa, inc=(p.sA, 1))
            s_tok = p.sA.n
            mw = [(p.sV, c_tok)]
            if dc == 0 and prev_evac:
                mw.append((p.sA, prev_evac))
            p.emit("tensor", lambda e, o=pm[:], l=ones_bf, r=xc_slot(i),
                   st=(dc == 0), sp=(dc == DC - 1):
                   e.matmul(out=o, lhsT=l, rhs=r, start=st, stop=sp), waits=mw)
            p.emit("tensor", lambda e, o=pv[:], l=ones_bf, r=sq_slot(i),
                   st=(dc == 0), sp=(dc == DC - 1):
                   e.matmul(out=o, lhsT=l, rhs=r, start=st, stop=sp),
                   waits=[(p.sA, s_tok)], inc=(p.sT, 1))
            mm_tok[i] = p.sT.n
        t_tok = p.sT.n
        p.emit("scalar", lambda e, o=m_bc[:, hs], a=pm[:]:
               e.mul(out=o, in_=a, mul=1.0 / D), waits=[(p.sT, t_tok)], inc=(p.sA, 1))
        p.emit("scalar", lambda e, o=r_bc[:, hs], a=pv[:]:
               e.mul(out=o, in_=a, mul=1.0 / D), inc=(p.sA, 1))
        a_tok = p.sA.n
        prev_evac = a_tok
        scr = ctx.arena[:, 2048 + tc * 512: 2048 + (tc + 1) * 512]
        p.emit("vector", lambda e, o=scr, a=m_bc[:, hs], b=m_bc[:, hs]:
               e.tensor_mul(out=o, in0=a, in1=b), waits=[(p.sA, a_tok)], inc=(p.sV, 1))
        p.emit("vector", lambda e, o=r_bc[:, hs], a=r_bc[:, hs], b=scr:
               e.tensor_sub(out=o, in0=a, in1=b), inc=(p.sV, 1))
        v_tok = p.sV.n
        p.emit("scalar", lambda e, o=r_bc[:, hs], a=r_bc[:, hs], b=ctx.sm32[:, 352:353]:
               e.activation(out=o, in_=a, func=AF.Ln, bias=b), waits=[(p.sV, v_tok)])
        p.emit("scalar", lambda e, o=r_bc[:, hs], a=r_bc[:, hs]:
               e.activation(out=o, in_=a, func=AF.Exp, scale=-0.5), inc=(p.sA, 1))
        r_tok = p.sA.n
        for dc in range(DC):
            if full:
                xo = ctx.xnf(dc)[:, th * TH + tc * 512: th * TH + (tc + 1) * 512]
            else:
                xo = ctx.xnh(th, dc)[:, tc * 512:(tc + 1) * 512]
            p.emit("vector", lambda e, o=scr, a=ctx.x(dc)[:, cs], b=m_bc[:, hs]:
                   e.tensor_sub(out=o, in0=a, in1=b),
                   waits=[(p.sA, r_tok)] if dc == 0 else [], inc=(p.sV, 1))
            p.emit("vector", lambda e, o=xo, a=scr, b=r_bc[:, hs]:
                   e.tensor_mul(out=o, in0=a, in1=b), inc=(p.sV, 1))
    ctx.guard[("gu_psum", 0)] = (p.sA, prev_evac)
    return p.sV.n


def emit_routing(ctx, th, xn_tok):
    p = ctx.p

    def pv(fn, waits=()):
        p.emit("vector", fn, waits=list(waits) + [(p.sV, p.sV.n)], inc=(p.sV, 1))
    gw = ctx.smbf[:, 256:320]
    logits = ctx.sm32[:, 88:216]
    WteT = ctx.sm32[:, 216:344]
    cols = ctx.sm32[:, 344:352]
    gw_tok = ctx.guard.get("gw_tok", 0)
    idf_tok = ctx.guard.get("idf_tok", 0)
    wte_read = ctx.guard.get("wte_read", 0)
    prev_exp = 0
    prev_cp = 0
    for g in range(8):
        pl = ctx.pb[2]
        for dc in range(DC):
            w0 = []
            if dc == 0:
                if g == 0:
                    w0 = [(p.sV, xn_tok), (p.sW, gw_tok)]
                if prev_exp:
                    w0.append((p.sA, prev_exp))
            p.emit("tensor", lambda e, o=pl[:, 0:8],
                   l=ctx.xnh(th, dc)[:, g * 128:(g + 1) * 128],
                   r=gw[:, dc * 8:dc * 8 + 8], st=(dc == 0), sp=(dc == DC - 1):
                   e.matmul(out=o, lhsT=l, rhs=r, start=st, stop=sp),
                   waits=w0, inc=(p.sT, 1) if dc == DC - 1 else None)
        mm_tok = p.sT.n
        gcol = slice(g * 8, g * 8 + 8)
        pv(lambda e, o=cols[:, 0:1], a=pl[:, 0:8]:
               e.reduce_max(out=o, in_=a, axis=AX, negate=True),
               waits=[(p.sT, mm_tok)])
        nm = p.sV.n
        p.emit("scalar", lambda e, o=logits[:, gcol], a=pl[:, 0:8], b=cols[:, 0:1], acc=cols[:, 1:2]:
               e.activation(out=o, in_=a, func=AF.Exp, bias=b, accum_out=acc),
               waits=[(p.sV, nm)], inc=(p.sA, 1))
        prev_exp = p.sA.n
        pv(lambda e, o=cols[:, 2:3], a=cols[:, 1:2]:
               e.reciprocal(out=o, in_=a), waits=[(p.sA, prev_exp)])
        p.emit("vector", lambda e, o=logits[:, gcol], a=logits[:, gcol], b=cols[:, 2:3]:
               e.tensor_scalar_mul(out=o, in0=a, scalar1=b))
        pv(lambda e, o=cols[:, 3:4], a=logits[:, gcol]:
               e.reduce_max(out=o, in_=a, axis=AX))
        pv(lambda e, o=WteT[:, gcol], a=logits[:, gcol], b=cols[:, 3:4]:
               e.tensor_scalar(out=o, in0=a, scalar1=b, scalar2=None, op0=ALU.is_ge))
        pv(lambda e, o=logits[:, gcol], a=WteT[:, gcol], b=logits[:, gcol]:
               e.scalar_tensor_tensor(out=o, in0=a, scalar=-2.0, in1=b,
                                      op0=ALU.mult, op1=ALU.add))
        pv(lambda e, o=cols[:, 4:5], a=logits[:, gcol]:
               e.reduce_max(out=o, in_=a, axis=AX))
        pv(lambda e, o=logits[:, gcol], a=logits[:, gcol], b=cols[:, 4:5]:
               e.tensor_scalar(out=o, in0=a, scalar1=b, scalar2=None, op0=ALU.is_ge))
        pv(lambda e, o=cols[:, 5:6], a=cols[:, 3:4], b=cols[:, 4:5]:
               e.tensor_add(out=o, in0=a, in1=b))
        pv(lambda e, o=cols[:, 5:6], a=cols[:, 5:6]:
               e.reciprocal(out=o, in_=a))
        pv(lambda e, o=cols[:, 6:7], a=cols[:, 3:4], b=cols[:, 5:6]:
               e.tensor_mul(out=o, in0=a, in1=b))
        pv(lambda e, o=cols[:, 6:7], a=cols[:, 6:7]:
               e.tensor_scalar_mul(out=o, in0=a, scalar1=0.5))
        pv(lambda e, o=cols[:, 7:8], a=cols[:, 4:5], b=cols[:, 5:6]:
               e.tensor_mul(out=o, in0=a, in1=b))
        pv(lambda e, o=cols[:, 7:8], a=cols[:, 7:8]:
               e.tensor_scalar_mul(out=o, in0=a, scalar1=0.5))
        pv(lambda e, o=WteT[:, gcol], a=WteT[:, gcol], b=cols[:, 6:7]:
               e.tensor_scalar_mul(out=o, in0=a, scalar1=b))
        pv(lambda e, o=WteT[:, gcol], a=logits[:, gcol], b=cols[:, 7:8], c=WteT[:, gcol]:
               e.scalar_tensor_tensor(out=o, in0=a, scalar=b, in1=c,
                                      op0=ALU.mult, op1=ALU.add))
        w_tok = p.sV.n
        tw = [(p.sV, w_tok)]
        if g == 0 and idf_tok:
            tw.append((p.sX, idf_tok))
        if prev_cp:
            tw.append((p.sA, prev_cp))
        p.emit("tensor", lambda e, o=ctx.pb[3][0:8, 0:128], i=WteT[:, gcol], idt=ctx.ident_f32[:]:
               e.transpose(out=o, in_=i, identity=idt), waits=tw, inc=(p.sT, 1))
        t_tok = p.sT.n
        cw = [(p.sT, t_tok)]
        if g == 0 and wte_read:
            cw.append((p.sT, wte_read))
        p.emit("scalar", lambda e, o=ctx.wte_em[0:8, g * 128:(g + 1) * 128],
               a=ctx.pb[3][0:8, 0:128]:
               e.copy(out=o, in_=a), waits=cw, inc=(p.sA, 1))
        prev_cp = p.sA.n
    ctx.guard[("gu_psum", 1)] = (p.sA, p.sA.n)
    return p.sA.n


FH = [(0, 768), (768, 1408)]
GU_OFF = [0, 12288]
WD_OFF = 22528


def emit_moe_half(ctx, wd_d, th, xn_tok, wte_tok, nexperts=10):
    p = ctx.p
    experts = ([("sh", i) for i in range(NSH)] + [("ex", i) for i in range(NEXP)])[:nexperts]
    sg_scr = [ctx.arena[:, 2048 + i * 512: 2048 + (i + 1) * 512] for i in range(2)]
    oh_tok = ctx.guard.get("oh_tok", 0)

    for ei, (kind, enum) in enumerate(experts):
        routed = kind == "ex"
        gname, uname, dname = (("sh_gate", "sh_up", "sh_down") if kind == "sh"
                               else ("exp_gate", "exp_up", "exp_down"))
        for fh in range(2):
            fs, fe = FH[fh]
            w = fe - fs
            off = GU_OFF[fh]
            waits = []
            pt = ctx.guard.get(("gu_done", fh), 0)
            if pt:
                waits = [(p.sT, pt)]
            eg = wd_d[gname][enum].rearrange("(dc p) f -> p dc f", p=128)[:, :, fs:fe]
            eu = wd_d[uname][enum].rearrange("(dc p) f -> p dc f", p=128)[:, :, fs:fe]
            p.dma("gpsimd", ctx.wbig[:, off:off + 8 * w]
                  .rearrange("p (dc f) -> p dc f", f=w), eg, waits=waits)
            t2 = p.dma("gpsimd", ctx.wbig[:, off + 8 * w:off + 16 * w]
                       .rearrange("p (dc f) -> p dc f", f=w), eu)
            ctx.guard[("gu_tok", fh)] = t2
        dwaits = []
        pt = ctx.guard.get("down_done", 0)
        if pt:
            dwaits = [(p.sT, pt)]
        ed = wd_d[dname][enum].rearrange("(fc p) d -> p fc d", p=128)
        t3 = p.dma("gpsimd", ctx.wbig[:, WD_OFF:WD_OFF + 11 * 1024]
                   .rearrange("p (fc d) -> p fc d", d=1024), ed, waits=dwaits)
        wd_tok = t3

        if routed:
            for tc in range(2):
                wv = [(p.sA, wte_tok)]
                if oh_tok:
                    wv.append((p.sW, oh_tok))
                prev = ctx.guard.get(("wte_free", tc), 0)
                if prev:
                    wv.append((p.sV, prev))
                p.emit("tensor", lambda e, o=ctx.pb[6 + tc][:],
                       l=ctx.onehot[:, enum * 128:(enum + 1) * 128],
                       r=ctx.wte_em[0:8, tc * 512:(tc + 1) * 512]:
                       e.matmul(out=o, lhsT=l, rhs=r, start=True, stop=True),
                       waits=wv, inc=(p.sT, 1))
                ctx.guard[("wte_bc", tc)] = p.sT.n

        down_prev = ctx.guard.get("down_done", 0)
        for fh in range(2):
            fs, fe = FH[fh]
            w = fe - fs
            nfc = w // 128
            off = GU_OFF[fh]
            gu_tok = ctx.guard[("gu_tok", fh)]
            for tc in range(2):
                for fcl in range(nfc):
                    fc = fs // 128 + fcl
                    slot = fcl % 2
                    pg = ctx.pb[slot * 2]
                    pu = ctx.pb[slot * 2 + 1]
                    mmw = [(p.sW, gu_tok), (p.sV, xn_tok)]
                    prev = ctx.guard.get(("gu_psum", slot))
                    if prev:
                        mmw.append(prev)
                    for dc in range(DC):
                        p.emit("tensor", lambda e, o=pg[:],
                               l=ctx.wbig[:, off + dc * w + fcl * 128: off + dc * w + fcl * 128 + 128],
                               r=ctx.xnh(th, dc)[:, tc * 512:(tc + 1) * 512],
                               st=(dc == 0), sp=(dc == DC - 1):
                               e.matmul(out=o, lhsT=l, rhs=r, start=st, stop=sp),
                               waits=mmw if dc == 0 else [])
                        p.emit("tensor", lambda e, o=pu[:],
                               l=ctx.wbig[:, off + 8 * w + dc * w + fcl * 128: off + 8 * w + dc * w + fcl * 128 + 128],
                               r=ctx.xnh(th, dc)[:, tc * 512:(tc + 1) * 512],
                               st=(dc == 0), sp=(dc == DC - 1):
                               e.matmul(out=o, lhsT=l, rhs=r, start=st, stop=sp),
                               inc=(p.sT, 1) if dc == DC - 1 else None)
                    gu_mm = p.sT.n
                    sw = [(p.sT, gu_mm)]
                    prevh = ctx.guard.get(("sg_free", slot), 0)
                    if prevh:
                        sw.append((p.sV, prevh))
                    p.emit("scalar", lambda e, o=sg_scr[slot], a=pg[:]:
                           e.activation(out=o, in_=a, func=AF.Silu),
                           waits=sw, inc=(p.sA, 1))
                    sg_tok = p.sA.n
                    hsl = ctx.hbig[:, fc * 1024 + tc * 512: fc * 1024 + (tc + 1) * 512]
                    hw = [(p.sA, sg_tok)]
                    if down_prev:
                        hw.append((p.sT, down_prev))
                    if routed:
                        hw.append((p.sT, ctx.guard[("wte_bc", tc)]))
                        p.emit("vector", lambda e, o=sg_scr[slot], a=sg_scr[slot], b=pu[:]:
                               e.tensor_mul(out=o, in0=a, in1=b), waits=hw, inc=(p.sV, 1))
                        p.emit("vector", lambda e, o=hsl, a=sg_scr[slot], b=ctx.pb[6 + tc][:]:
                               e.tensor_mul(out=o, in0=a, in1=b), inc=(p.sV, 1))
                    else:
                        p.emit("vector", lambda e, o=hsl, a=sg_scr[slot], b=pu[:]:
                               e.scalar_tensor_tensor(out=o, in0=a, scalar=0.5, in1=b,
                                                      op0=ALU.mult, op1=ALU.mult),
                               waits=hw, inc=(p.sV, 1))
                    ctx.guard[("gu_psum", slot)] = (p.sV, p.sV.n)
                    ctx.guard[("sg_free", slot)] = p.sV.n
            ctx.guard[("gu_done", fh)] = p.sT.n
        h_tok = p.sV.n
        if routed:
            for tc in range(2):
                ctx.guard[("wte_free", tc)] = h_tok

        for tc in range(2):
            cs = slice(th * TH + tc * 512, th * TH + (tc + 1) * 512)
            for dc in range(DC):
                pdw = ctx.pb[4 + dc % 2]
                dw = [(p.sV, h_tok), (p.sW, wd_tok)]
                prev = ctx.guard.get(("dw_psum", dc % 2), 0)
                if prev:
                    dw.append((p.sV, prev))
                for fc in range(11):
                    p.emit("tensor", lambda e, o=pdw[:],
                           l=ctx.wbig[:, WD_OFF + fc * 1024 + dc * 128: WD_OFF + fc * 1024 + dc * 128 + 128],
                           r=ctx.hbig[:, fc * 1024 + tc * 512: fc * 1024 + (tc + 1) * 512],
                           st=(fc == 0), sp=(fc == 10):
                           e.matmul(out=o, lhsT=l, rhs=r, start=st, stop=sp),
                           waits=dw if fc == 0 else [],
                           inc=(p.sT, 1) if fc == 10 else None)
                dmm = p.sT.n
                p.emit("vector", lambda e, o=ctx.x(dc)[:, cs], a=ctx.x(dc)[:, cs], b=pdw[:]:
                       e.tensor_add(out=o, in0=a, in1=b),
                       waits=[(p.sT, dmm)], inc=(p.sV, 1))
                ctx.guard[("dw_psum", dc % 2)] = p.sV.n
                ctx.x_tok[dc] = p.sV.n
        ctx.guard["down_done"] = p.sT.n
    ctx.guard["hbig_free"] = ctx.guard["down_done"]
    ctx.guard["wte_read"] = ctx.guard["down_done"]


def emit_moe_sublayer(ctx, wd_d, stage=99, nexperts=10, nth=2):
    for th in range(nth):
        xn_tok = emit_ln(ctx, th)
        if stage < 2:
            continue
        wte_tok = emit_routing(ctx, th, xn_tok)
        if stage < 3:
            continue
        emit_moe_half(ctx, wd_d, th, xn_tok, wte_tok, nexperts=nexperts)


def declare_moe_weights(nc):
    wd = {}
    wd["exp_gate"] = nc.dram_tensor("exp_gate", [NEXP, D, F_MOE], F32, kind="ExternalInput").ap()
    wd["exp_up"] = nc.dram_tensor("exp_up", [NEXP, D, F_MOE], F32, kind="ExternalInput").ap()
    wd["exp_down"] = nc.dram_tensor("exp_down", [NEXP, F_MOE, D], F32, kind="ExternalInput").ap()
    wd["sh_gate"] = nc.dram_tensor("sh_gate", [NSH, D, F_MOE], F32, kind="ExternalInput").ap()
    wd["sh_up"] = nc.dram_tensor("sh_up", [NSH, D, F_MOE], F32, kind="ExternalInput").ap()
    wd["sh_down"] = nc.dram_tensor("sh_down", [NSH, F_MOE, D], F32, kind="ExternalInput").ap()
    return wd


def emit_prologue(ctx, xin, need_moe=True):
    p = ctx.p
    nc = ctx.nc
    ident_d = nc.dram_tensor("ident", [128, 128], F32, kind="ExternalInput").ap()
    p.dma("sync", ctx.ident_f32[:], ident_d)
    for dc in range(DC):
        p.dma("sync", ctx.x(dc), xin[dc * 128:(dc + 1) * 128, :])
    ctx.guard["xload"] = p.sX.n
    ctx.guard["idf_tok"] = p.sX.n
    p.dma("gpsimd", ctx.smbf[:, 128:256], ident_d)
    if need_moe:
        gate_w_sb_d = nc.dram_tensor("gate_w_sb", [128, 64], F32, kind="ExternalInput").ap()
        onehot_d = nc.dram_tensor("onehot_bf", [8, 1024], F32, kind="ExternalInput").ap()
        p.dma("gpsimd", ctx.smbf[:, 256:320], gate_w_sb_d)
        p.dma("gpsimd", ctx.onehot[:], onehot_d)
    ctx.guard["gw_tok"] = p.sW.n
    ctx.guard["oh_tok"] = p.sW.n
    tid = p.sW.n
    p.emit("vector", lambda e: e.memset(ctx.smbf[:, 0:128], 1.0),
           waits=[(p.sX, ctx.guard["xload"]), (p.sW, tid)])
    p.emit("vector", lambda e: e.memset(ctx.sm32[:, 352:353], 1e-5), inc=(p.sV, 1))
    ctx.guard["consts_ready"] = p.sV.n


def emit_epilogue(ctx, xout):
    p = ctx.p
    toks = []
    for dc in range(DC):
        w = [(p.sV, ctx.x_tok[dc])]
        if dc == 0:
            w.append((p.sX, ctx.guard["xload"]))
        toks.append(p.dma("sync", xout[dc * 128:(dc + 1) * 128, :], ctx.x(dc), waits=w))
    p.wait("sync", p.sX, toks[-1])


def build_nc_moe(stage=99, nexperts=10, nth=2, debug=False):
    nc = bass.Bass("TRN2", target_bir_lowering=False, detect_race_conditions=False)
    p = Prog(nc)
    xin = nc.dram_tensor("xin", [D, T], F32, kind="ExternalInput").ap()
    xout = nc.dram_tensor("xout", [D, T], F32, kind="ExternalOutput").ap()
    wd = declare_moe_weights(nc)
    ctx = Ctx(nc, p)
    emit_prologue(ctx, xin)
    if stage >= 1:
        emit_moe_sublayer(ctx, wd, stage=stage, nexperts=nexperts, nth=nth)
    if debug:
        xnd = nc.dram_tensor("xn_dump", [128, DC * T], F32, kind="ExternalOutput").ap()
        wted = nc.dram_tensor("wte_dump", [8, 1024], F32, kind="ExternalOutput").ap()
        hd = nc.dram_tensor("h_dump", [128, 11264], F32, kind="ExternalOutput").ap()
        dbg_sb = ctx.arena[:, 0:2048]
        vtok0 = p.sV.n
        for blk in range(4 * nth):
            sl = slice(blk * 2048, (blk + 1) * 2048)
            p.emit("vector", lambda e, o=dbg_sb, a=ctx.xnbig[:, sl]:
                   e.tensor_copy(out=o, in_=a),
                   waits=[(p.sV, ctx.x_tok[0]), (p.sX, p.sX.n)], inc=(p.sV, 1))
            p.dma("sync", xnd[:, sl], dbg_sb, waits=[(p.sV, p.sV.n)])
        for blk in range(8):
            sl = slice(blk * 1408, (blk + 1) * 1408)
            p.emit("vector", lambda e, o=ctx.arena[:, 0:1408], a=ctx.hbig[:, sl]:
                   e.tensor_copy(out=o, in_=a),
                   waits=[(p.sX, p.sX.n)], inc=(p.sV, 1))
            p.dma("sync", hd[:, sl], ctx.arena[:, 0:1408], waits=[(p.sV, p.sV.n)])
        p.emit("vector", lambda e, o=ctx.arena[0:8, 0:1024], a=ctx.wte_em[:]:
               e.tensor_copy(out=o, in_=a), waits=[(p.sX, p.sX.n)], inc=(p.sV, 1))
        p.dma("sync", wted, ctx.arena[0:8, 0:1024], waits=[(p.sV, p.sV.n)])
    emit_epilogue(ctx, xout)
    p.build()
    return nc


def _host_constants():
    ident = np.eye(128, dtype=np.float32)
    onehot = np.zeros((8, 1024), np.float32)
    for e in range(8):
        onehot[e, e * 128:(e + 1) * 128] = 1.0
    return ident, onehot


def _rearrange_gate_w(gate_w):
    return np.ascontiguousarray(gate_w.reshape(8, 128, 8).transpose(1, 0, 2).reshape(128, 64))


def moe_in_maps(x_full, inputs):
    ident, onehot = _host_constants()
    gw = _rearrange_gate_w(np.asarray(inputs["L0_ffn_gate_w"], np.float32))
    maps = []
    for b in range(NCORE):
        maps.append({
            "xin": np.ascontiguousarray(np.asarray(x_full[b], np.float32).T),
            "gate_w_sb": gw,
            "onehot_bf": onehot,
            "ident": ident,
            "exp_gate": np.asarray(inputs["L0_ffn_exp_gate"], np.float32),
            "exp_up": np.asarray(inputs["L0_ffn_exp_up"], np.float32),
            "exp_down": np.asarray(inputs["L0_ffn_exp_down"], np.float32),
            "sh_gate": np.asarray(inputs["L0_ffn_sh_gate"], np.float32),
            "sh_up": np.asarray(inputs["L0_ffn_sh_up"], np.float32),
            "sh_down": np.asarray(inputs["L0_ffn_sh_down"], np.float32),
        })
    return maps


def run_moe_phase(x_full, inputs, stage=99, nexperts=10, nth=2):
    nc = build_nc_moe(stage=stage, nexperts=nexperts, nth=nth)
    res = run_bass_kernel_spmd(nc, moe_in_maps(x_full, inputs), core_ids=list(range(NCORE)))
    return np.stack([np.ascontiguousarray(r["xout"].T) for r in res.results])


def emit_attn0(ctx, wd):
    p = ctx.p
    QF, KF = 0, 16384
    WS = 5184
    xn_tok = ctx._attn_xn_tok
    hb_guard = ctx.guard.get("hbig_free", 0)
    p.dma("sync", ctx.sm32[:, 0:8], wd["bq_sb"])
    p.dma("sync", ctx.sm32[:, 8:16], wd["bk_sb"])
    btok = p.sX.n

    wname = {0: "Wq", 1: "Wk"}
    for pi, (ooff, boff) in enumerate(((QF, 0), (KF, 8))):
        for fc in range(8):
            i = pi * 8 + fc
            ws = ctx.hbig[:, WS + (i % 2) * 1024: WS + (i % 2) * 1024 + 1024]
            lw = []
            prev = ctx.guard.get(("a0w", i % 2))
            if prev:
                lw.append(prev)
            elif hb_guard:
                lw.append((p.sT, hb_guard))
            wtok = p.dma("gpsimd", ws.rearrange("p (dc f) -> p dc f", f=128),
                         wd[wname[pi]].rearrange("(dc p) f -> p dc f", p=128)[:, :, fc * 128:(fc + 1) * 128],
                         waits=lw)
            for qc in range(4):
                ps = ctx.pb[qc % 2]
                w0 = [(p.sW, wtok), (p.sV, xn_tok)]
                prev = ctx.guard.get(("a0p", qc % 2))
                if prev:
                    w0.append(prev)
                for dc in range(DC):
                    p.emit("tensor", lambda e, o=ps[:],
                           l=ws[:, dc * 128:(dc + 1) * 128],
                           r=ctx.xnf(dc)[:, qc * 512:(qc + 1) * 512],
                           st=(dc == 0), sp=(dc == DC - 1):
                           e.matmul(out=o, lhsT=l, rhs=r, start=st, stop=sp),
                           waits=w0 if dc == 0 else [],
                           inc=(p.sT, 1) if dc == DC - 1 else None)
                mm = p.sT.n
                if qc == 3:
                    ctx.guard[("a0w", i % 2)] = (p.sT, mm)
                p.emit("scalar", lambda e, o=ctx.wbig[:, ooff + fc * 2048 + qc * 512: ooff + fc * 2048 + (qc + 1) * 512],
                       a=ps[:], b=ctx.sm32[:, boff + fc:boff + fc + 1]:
                       e.add(out=o, in_=a, add=b),
                       waits=[(p.sT, mm), (p.sX, btok)], inc=(p.sA, 1))
                ctx.guard[("a0p", qc % 2)] = (p.sA, p.sA.n)
    qk_tok = p.sA.n

    def q_head(h, sl):
        return ctx.wbig[(h % 2) * 64:(h % 2) * 64 + 64, QF + (h // 2) * 2048 + sl.start: QF + (h // 2) * 2048 + sl.stop]

    def k_head(h, sl):
        return ctx.wbig[(h % 2) * 64:(h % 2) * 64 + 64, KF + (h // 2) * 2048 + sl.start: KF + (h // 2) * 2048 + sl.stop]

    hw = [(p.sT, hb_guard)] if hb_guard else []
    for kb in range(2):
        p.emit("vector", lambda e, kb=kb:
               e.memset(ctx.hbig[:, kb * 1040:(kb + 1) * 1040]
                        .rearrange("p (kc c) -> p kc c", c=65)[:, :, 64:65], 1.0),
               waits=hw, inc=(p.sV, 1))
    ones_tok = p.sV.n

    for h in range(NH):
        kb = h % 2
        kaug = ctx.hbig[:, kb * 1040:(kb + 1) * 1040]
        prev_use = ctx.guard.get(("kaug", kb))
        pp0 = (h % 2) * 64
        idt_sl = ctx.smbf[pp0:pp0 + 64, 128 + pp0:128 + pp0 + 64]
        for kc in range(16):
            tw = [(p.sA, qk_tok), (p.sV, ones_tok)]
            prev_t = ctx.guard.get("a0_tr")
            if prev_t:
                tw.append(prev_t)
            p.emit("tensor", lambda e, o=ctx.pb[3][:].bitcast(BF16)[:, 0:64],
                   i=k_head(h, slice(kc * 128, kc * 128 + 128)), idt=idt_sl:
                   e.transpose(out=o, in_=i, identity=idt),
                   waits=tw, inc=(p.sT, 1))
            ttok = p.sT.n
            cw = [(p.sT, ttok)]
            if prev_use and kc == 0:
                cw.append(prev_use)
            p.emit("scalar", lambda e, o=kaug[:, kc * 65:kc * 65 + 64], a=ctx.pb[3][:].bitcast(BF16)[:, 0:64]:
                   e.copy(out=o, in_=a), waits=cw, inc=(p.sA, 1))
            ctx.guard["a0_tr"] = (p.sA, p.sA.n)
        kaug_tok = p.sA.n

        for qc in range(4):
            qsl = slice(qc * 512, (qc + 1) * 512)
            po = ctx.pb[4 + qc % 2]
            prev_o = ctx.guard.get(("a0out", qc % 2))
            for kc in range(16):
                pst = ctx.pb[kc % 2]
                es = ctx.hbig[:, 2080 + (kc % 2) * 512: 2080 + (kc % 2) * 512 + 512]
                stw = [(p.sA, kaug_tok)]
                prev_e = ctx.guard.get(("a0p2", kc % 2))
                if prev_e:
                    stw.append(prev_e)
                p.emit("tensor", lambda e, o=pst[:], l=k_head(h, slice(kc * 128, kc * 128 + 128)),
                       r=q_head(h, qsl):
                       e.matmul(out=o, lhsT=l, rhs=r, start=True, stop=True),
                       waits=stw, inc=(p.sT, 1))
                st_tok = p.sT.n
                ew0 = [(p.sT, st_tok)]
                prev_es = ctx.guard.get(("a0e", kc % 2))
                if prev_es:
                    ew0.append(prev_es)
                p.emit("scalar", lambda e, o=es, a=pst[:]:
                       e.activation(out=o, in_=a, func=AF.Exp, scale=1.0 / 32.0),
                       waits=ew0, inc=(p.sA, 1))
                e_tok = p.sA.n
                ctx.guard[("a0p2", kc % 2)] = (p.sA, e_tok)
                vw = [(p.sA, e_tok)]
                if kc == 0 and prev_o:
                    vw.append(prev_o)
                p.emit("tensor", lambda e, o=po[0:65, :], l=kaug[:, kc * 65:(kc + 1) * 65],
                       r=es, st=(kc == 0), sp=(kc == 15):
                       e.matmul(out=o, lhsT=l, rhs=r, start=st, stop=sp),
                       waits=vw, inc=(p.sT, 1))
                ctx.guard[("a0e", kc % 2)] = (p.sT, p.sT.n)
            o_tok = p.sT.n
            p.emit("vector", lambda e, o=ctx.arena[64:65, 1024:1536], a=po[64:65, :]:
                   e.reciprocal(out=o, in_=a), waits=[(p.sT, o_tok)], inc=(p.sV, 1))
            p.emit("vector", lambda e, o=ctx.smbf[64:65, 512:1024], a=ctx.arena[64:65, 1024:1536]:
                   e.tensor_copy(out=o, in_=a), waits=[(p.sV, p.sV.n)], inc=(p.sV, 1))
            r_tok = p.sV.n
            prev_rb = ctx.guard.get("a0rb")
            rw = [(p.sV, r_tok)]
            if prev_rb:
                rw.append(prev_rb)
            p.emit("tensor", lambda e, o=ctx.pb[6][:], l=ctx.smbf[64:65, 0:128],
                   r=ctx.smbf[64:65, 512:1024]:
                   e.matmul(out=o, lhsT=l, rhs=r, start=True, stop=True),
                   waits=rw, inc=(p.sT, 1))
            rb_tok = p.sT.n
            p.emit("scalar", lambda e, o=ctx.arena[0:64, 0:512], a=po[0:64, :]:
                   e.copy(out=o, in_=a), waits=[(p.sT, rb_tok)], inc=(p.sA, 1))
            ev_tok = p.sA.n
            dc = h // 2
            pp = (h % 2) * 64
            if pp == 0:
                p.emit("vector", lambda e, o=ctx.arena[0:64, 512:1024], a=ctx.arena[0:64, 0:512], b=ctx.pb[6][0:64, :]:
                       e.tensor_mul(out=o, in0=a, in1=b),
                       waits=[(p.sA, ev_tok)], inc=(p.sV, 1))
                y_tok = p.sV.n
                ctx.guard["a0rb"] = (p.sV, y_tok)
                p.emit("vector", lambda e, o=ctx.x(dc)[0:64, qsl], a=ctx.x(dc)[0:64, qsl],
                       b=ctx.arena[0:64, 512:1024]:
                       e.tensor_add(out=o, in0=a, in1=b),
                       waits=[(p.sV, y_tok)], inc=(p.sV, 1))
            else:
                p.emit("vector", lambda e, o=ctx.smbf[0:64, 512:1024], a=ctx.arena[0:64, 0:512], b=ctx.pb[6][0:64, :]:
                       e.tensor_mul(out=o, in0=a, in1=b),
                       waits=[(p.sA, ev_tok)], inc=(p.sV, 1))
                y_tok = p.sV.n
                ctx.guard["a0rb"] = (p.sV, y_tok)
                prev_rl = ctx.guard.get("a0rl")
                rl = [(p.sV, y_tok)]
                if prev_rl:
                    rl.append(prev_rl)
                p.emit("tensor", lambda e, o=ctx.pb[7][64:128, :], l=ctx.smbf[0:64, 128:192],
                       r=ctx.smbf[0:64, 512:1024]:
                       e.matmul(out=o, lhsT=l, rhs=r, start=True, stop=True,
                                tile_position=(0, 64)),
                       waits=rl, inc=(p.sT, 1))
                rl_tok = p.sT.n
                p.emit("vector", lambda e, o=ctx.x(dc)[64:128, qsl], a=ctx.x(dc)[64:128, qsl],
                       b=ctx.pb[7][64:128, :]:
                       e.tensor_add(out=o, in0=a, in1=b),
                       waits=[(p.sT, rl_tok)], inc=(p.sV, 1))
                ctx.guard["a0rl"] = (p.sV, p.sV.n)
            ctx.x_tok[dc] = p.sV.n
            ctx.guard[("a0out", qc % 2)] = (p.sA, ev_tok)
        ctx.guard[("kaug", kb)] = (p.sT, p.sT.n)
    ctx.guard["hbig_free"] = p.sT.n
    ctx.guard["gu_done"] = 0
    ctx.guard["down_done"] = p.sT.n


def emit_attn0_sublayer(ctx, wd):
    t0 = emit_ln(ctx, 0, full=True)
    t1 = emit_ln(ctx, 1, full=True)
    ctx._attn_xn_tok = t1
    emit_attn0(ctx, wd)


def declare_attn0_weights(nc):
    wd = {}
    wd["Wq"] = nc.dram_tensor("Wq", [D, D], F32, kind="ExternalInput").ap()
    wd["Wk"] = nc.dram_tensor("Wk", [D, D], F32, kind="ExternalInput").ap()
    wd["bq_sb"] = nc.dram_tensor("bq_sb", [128, 8], F32, kind="ExternalInput").ap()
    wd["bk_sb"] = nc.dram_tensor("bk_sb", [128, 8], F32, kind="ExternalInput").ap()
    return wd


def build_nc_attn0():
    nc = bass.Bass("TRN2", target_bir_lowering=False, detect_race_conditions=False)
    p = Prog(nc)
    xin = nc.dram_tensor("xin", [D, T], F32, kind="ExternalInput").ap()
    xout = nc.dram_tensor("xout", [D, T], F32, kind="ExternalOutput").ap()
    wd = declare_attn0_weights(nc)
    ctx = Ctx(nc, p)
    emit_prologue(ctx, xin, need_moe=False)
    emit_attn0_sublayer(ctx, wd)
    emit_epilogue(ctx, xout)
    p.build()
    return nc


def attn0_in_maps(x_full, inputs):
    ident, _ = _host_constants()
    bq = np.asarray(inputs["L0_attn_bq"], np.float32).reshape(8, 128).T.copy()
    bk = np.asarray(inputs["L0_attn_bk"], np.float32).reshape(8, 128).T.copy()
    maps = []
    for b in range(NCORE):
        maps.append({
            "xin": np.ascontiguousarray(np.asarray(x_full[b], np.float32).T),
            "ident": ident,
            "Wq": np.asarray(inputs["L0_attn_Wq"], np.float32),
            "Wk": np.asarray(inputs["L0_attn_Wk"], np.float32),
            "bq_sb": np.ascontiguousarray(bq),
            "bk_sb": np.ascontiguousarray(bk),
        })
    return maps


def run_attn0_phase(x_full, inputs):
    nc = build_nc_attn0()
    res = run_bass_kernel_spmd(nc, attn0_in_maps(x_full, inputs), core_ids=list(range(NCORE)))
    return np.stack([np.ascontiguousarray(r["xout"].T) for r in res.results])


MAGIC = float(2 ** 22)


def emit_quant(ctx, dst, src, W, scale_const):
    p = ctx.p
    nb = W // 16
    y = ctx.arena[:, 0:W]
    a = ctx.arena[:, 512:512 + W]
    c = ctx.arena[:, 1024:1024 + W]
    q = ctx.arena[:, 1536:1536 + W]
    bm = ctx.sm32[:, 64:64 + nb]
    rs = ctx.sm32[:, 96:96 + nb]

    def pv(fn, waits=()):
        p.emit("vector", fn, waits=list(waits) + [(p.sV, p.sV.n)], inc=(p.sV, 1))

    d3 = lambda ap: ap.rearrange("p (nb b) -> p nb b", b=16)
    pv(lambda e, o=bm.unsqueeze(2), i=d3(src):
       e.reduce_max(out=o, in_=i, axis=AX, apply_absolute_value=True))
    pv(lambda e, o=rs, i=bm: e.reciprocal(out=o, in_=i))
    pv(lambda e, o=d3(y), i=d3(src), b=rs.unsqueeze(2).broadcast_to([128, nb, 16]):
       e.scalar_tensor_tensor(out=o, in0=i, scalar=6.0, in1=b, op0=ALU.mult, op1=ALU.mult))
    y_tok = p.sV.n
    aw = [(p.sV, y_tok)]
    prev = ctx.guard.get("qact")
    if prev:
        aw.append(prev)
    p.emit("scalar", lambda e, o=a, i=y: e.activation(out=o, in_=i, func=AF.Abs),
           waits=aw, inc=(p.sA, 1))
    abs_tok = p.sA.n
    pv(lambda e, o=q, i=a: e.tensor_scalar_add(out=o, in0=i, scalar1=MAGIC),
       waits=[(p.sA, abs_tok)])
    pv(lambda e, o=q, i=q: e.tensor_scalar(out=o, in0=i, scalar1=MAGIC, scalar2=2.0,
                                           op0=ALU.subtract, op1=ALU.min))
    pv(lambda e, o=c, i=a: e.tensor_single_scalar(out=o, in_=i, scalar=2.5, op=ALU.is_gt))
    pv(lambda e, o=c, i=a, b=c: e.scalar_tensor_tensor(out=o, in0=i, scalar=3.5, in1=b,
                                                       op0=ALU.is_gt, op1=ALU.add))
    pv(lambda e, o=c, i=a, b=c: e.scalar_tensor_tensor(out=o, in0=i, scalar=5.0, in1=b,
                                                       op0=ALU.is_gt, op1=ALU.add))
    pv(lambda e, o=c, i=a, b=c: e.scalar_tensor_tensor(out=o, in0=i, scalar=5.0, in1=b,
                                                       op0=ALU.is_gt, op1=ALU.add))
    pv(lambda e, o=q, i=q, b=c: e.tensor_add(out=o, in0=i, in1=b))
    q_tok = p.sV.n
    p.emit("scalar", lambda e, o=c, i=y: e.sign(out=o, in_=i),
           waits=[(p.sV, q_tok)], inc=(p.sA, 1))
    sgn_tok = p.sA.n
    ctx.guard["qact"] = (p.sV, 0)
    pv(lambda e, o=q, i=q, b=c: e.tensor_mul(out=o, in0=i, in1=b),
       waits=[(p.sA, sgn_tok)])
    pv(lambda e, o=d3(dst), i=d3(q), b=bm.unsqueeze(2).broadcast_to([128, nb, 16]),
       s=scale_const:
       e.scalar_tensor_tensor(out=o, in0=i, scalar=s, in1=b, op0=ALU.mult, op1=ALU.mult))
    ctx.guard["qact"] = (p.sV, p.sV.n)
    return p.sV.n


def emit_wprep(ctx, wd):
    p = ctx.p
    wf = ctx.wbig[:].bitcast(F32)
    jobs = [("w_gate", "wgq_nat", 22, 1024, 1.0 / 6.0),
            ("w_up", "wuq_nat", 22, 1024, 1.0 / 6.0),
            ("w_down", "wdq_nat", 8, 2816, 1.0 / 12.0)]
    i = 0
    for src_name, dst_name, nch, W, sc in jobs:
        for ch in range(nch):
            sl = i % 2
            fsrc = wf[:, sl * 2816: sl * 2816 + W]
            bdst = ctx.wbig[:, 11264 + sl * 3072: 11264 + sl * 3072 + W]
            lw = []
            prev = ctx.guard.get(("wp", sl))
            if prev:
                lw.append(prev)
            lt = p.dma("sync", fsrc, wd[src_name][ch * 128:(ch + 1) * 128, :], waits=lw)
            p.wait("vector", p.sX, lt)
            for off in range(0, W, 512):
                w = min(512, W - off)
                emit_quant(ctx, bdst[:, off:off + w], fsrc[:, off:off + w], w, sc)
            qt = p.sV.n
            ctx.guard[("wp", sl)] = (p.sV, qt)
            st = p.dma("sync", wd[dst_name][ch * 128:(ch + 1) * 128, :], bdst,
                       waits=[(p.sV, qt)])
            ctx.guard[("wp_st", sl)] = st
            i += 1
    ctx.guard["wprep_done"] = p.sX.n


GGR = [(0, 512), (512, 1024), (1024, 1536), (1536, 2048), (2048, 2560), (2560, 2816)]


def emit_ffn1(ctx, wd, tag, stage=99, banks=(0, 2, 6)):
    p = ctx.p
    if stage == 10:
        return
    wprep = ctx.guard.get("wprep_done", 0)
    hb = ctx.hbig[:].bitcast(F32)
    t32b, tbfb, pdwb = banks

    hb_guard = ctx.guard.get("hbig_free", 0)
    xn_guard = ctx.guard.get("xn_free", 0)
    pb_base = t32b
    for tg in range(16):
        tsl = slice(tg * 128, (tg + 1) * 128)
        x_tm = ctx.arena[:, 2048:3072]
        for dc in range(DC):
            tw = [(p.sV, ctx.x_tok[dc])] if tg == 0 else []
            prev = ctx.guard.get(("f1tr", dc % 2))
            if prev:
                tw.append(prev)
            prev_bq = ctx.guard.get(("f1tq", dc % 2))
            if prev_bq:
                tw.append(prev_bq)
            if tg == 0 and dc == 0:
                if hb_guard:
                    tw.append((p.sT, hb_guard))
                tw.append((p.sX, ctx.guard.get("xload", 0)))
                tw.append((p.sV, ctx.guard.get("consts_ready", 0)))
            p.emit("tensor", lambda e, o=ctx.pb[pb_base + dc % 2][:, 0:128], i=ctx.x(dc)[:, tsl],
                   idt=ctx.ident_f32[:]:
                   e.transpose(out=o, in_=i, identity=idt), waits=tw, inc=(p.sT, 1))
            ttok = p.sT.n
            if stage == 21:
                ctx.guard[("f1tr", dc % 2)] = (p.sT, ttok)
                continue
            cw = [(p.sT, ttok)]
            prev_q = ctx.guard.get("f1xtm")
            if prev_q and dc == 0:
                cw.append(prev_q)
            p.emit("scalar", lambda e, o=x_tm[:, dc * 128:(dc + 1) * 128], a=ctx.pb[pb_base + dc % 2][:, 0:128]:
                   e.copy(out=o, in_=a), waits=cw, inc=(p.sA, 1))
            ctx.guard[("f1tr", dc % 2)] = (p.sA, p.sA.n)
        if stage == 21:
            continue
        xtm_tok = p.sA.n
        if stage in (11, 22):
            ctx.guard["f1xtm"] = (p.sA, xtm_tok)
            continue
        s_col = ctx.sm32[:, 128:129]
        v_col = ctx.sm32[:, 129:130]
        r_col = ctx.sm32[:, 130:131]
        dump = hb[:, 4096:5120]
        p.emit("vector", lambda e, o=s_col, i=x_tm:
               e.reduce_sum(out=o, in_=i, axis=AX),
               waits=[(p.sA, xtm_tok), (p.sV, p.sV.n)], inc=(p.sV, 1))
        sv = p.sV.n
        dw = [(p.sA, xtm_tok)]
        prev_d = ctx.guard.get("f1dump")
        if prev_d:
            dw.append(prev_d)
        p.emit("scalar", lambda e, o=dump, i=x_tm, acc=v_col:
               e.activation(out=o, in_=i, func=AF.Square, accum_out=acc),
               waits=dw, inc=(p.sA, 1))
        sq_tok = p.sA.n
        ctx.guard["f1dump"] = (p.sA, sq_tok)
        p.emit("vector", lambda e, o=s_col, i=s_col:
               e.tensor_scalar_mul(out=o, in0=i, scalar1=1.0 / D),
               waits=[(p.sV, sv)], inc=(p.sV, 1))
        p.emit("vector", lambda e, o=v_col, i=v_col:
               e.tensor_scalar_mul(out=o, in0=i, scalar1=1.0 / D),
               waits=[(p.sA, sq_tok), (p.sV, p.sV.n)], inc=(p.sV, 1))
        p.emit("vector", lambda e, o=r_col, i=s_col, b=s_col:
               e.tensor_mul(out=o, in0=i, in1=b), waits=[(p.sV, p.sV.n)], inc=(p.sV, 1))
        p.emit("vector", lambda e, o=v_col, i=v_col, b=r_col:
               e.tensor_sub(out=o, in0=i, in1=b), waits=[(p.sV, p.sV.n)], inc=(p.sV, 1))
        vv = p.sV.n
        p.emit("scalar", lambda e, o=r_col, i=v_col, b=ctx.sm32[:, 352:353]:
               e.activation(out=o, in_=i, func=AF.Ln, bias=b),
               waits=[(p.sV, vv)], inc=(p.sA, 1))
        p.emit("scalar", lambda e, o=r_col, i=r_col:
               e.activation(out=o, in_=i, func=AF.Exp, scale=-0.5), inc=(p.sA, 1))
        rtok = p.sA.n
        p.emit("vector", lambda e, o=x_tm, i=x_tm, m=s_col, r=r_col:
               e.tensor_scalar(out=o, in0=i, scalar1=m, scalar2=r,
                               op0=ALU.subtract, op1=ALU.mult),
               waits=[(p.sA, rtok), (p.sV, p.sV.n)], inc=(p.sV, 1))
        if stage == 12:
            ctx.guard["f1xtm"] = (p.sV, p.sV.n)
            continue
        xq_tm = ctx.hbig[:, 10240:11264]
        for off in (0, 512):
            emit_quant(ctx, xq_tm[:, off:off + 512], x_tm[:, off:off + 512], 512, 1.0 / 6.0)
        q_tok = p.sV.n
        ctx.guard["f1xtm"] = (p.sV, q_tok)
        if stage == 13:
            continue
        for dc in range(DC):
            tw = [(p.sV, q_tok)]
            prev = ctx.guard.get(("f1tq", dc % 2))
            if prev:
                tw.append(prev)
            if tg == 0 and dc == 0 and xn_guard:
                tw.append((p.sT, xn_guard))
            p.emit("tensor", lambda e, o=ctx.pb[tbfb + dc % 2][:].bitcast(BF16)[:, 0:128],
                   i=xq_tm[:, dc * 128:(dc + 1) * 128], idt=ctx.smbf[:, 128:256]:
                   e.transpose(out=o, in_=i, identity=idt), waits=tw, inc=(p.sT, 1))
            ttok = p.sT.n
            p.emit("scalar", lambda e, o=ctx.xnf(dc)[:, tsl], a=ctx.pb[tbfb + dc % 2][:].bitcast(BF16)[:, 0:128]:
                   e.copy(out=o, in_=a), waits=[(p.sT, ttok)], inc=(p.sA, 1))
            ctx.guard[("f1tq", dc % 2)] = (p.sA, p.sA.n)
    xq_tok = p.sA.n
    if stage < 2 or stage in (11, 12, 13, 21, 22):
        return

    dw = []
    prev = ctx.guard.get("wdq_free")
    if prev:
        dw.append(prev)
    if wprep:
        dw.append((p.sX, wprep))
    for fc in range(22):
        p.dma("sync", ctx.wbig[:, fc * 1024:(fc + 1) * 1024],
              wd["wdq_nat"][:, fc * 128:(fc + 1) * 128], waits=dw if fc == 0 else (),
              transpose=True)
    wdq_tok = p.sX.n

    for gi, (fs, fe) in enumerate(GGR):
        w = fe - fs
        sl = gi % 2
        wg_sb = ctx.wbig[:, 22528 + sl * 4096: 22528 + sl * 4096 + 8 * 512]
        wu_sb = ctx.hbig[:, sl * 4096: sl * 4096 + 8 * 512]
        lw = []
        prev = ctx.guard.get(("f1g", sl))
        if prev:
            lw.append(prev)
        if gi == 0 and wprep:
            lw.append((p.sX, wprep))
        for hc in range(DC):
            p.dma("sync", wg_sb[:, hc * 512:hc * 512 + w],
                  wd["wgq_nat"][fs:fe, hc * 128:(hc + 1) * 128],
                  waits=lw if hc == 0 else (), transpose=True)
            p.dma("sync", wu_sb[:, hc * 512:hc * 512 + w],
                  wd["wuq_nat"][fs:fe, hc * 128:(hc + 1) * 128], transpose=True)
        gran_tok = p.sX.n
        for tg in range(16):
            tsl = slice(tg * 128, (tg + 1) * 128)
            pg = ctx.pb[4]
            pu = ctx.pb[5]
            mw = [(p.sX, gran_tok), (p.sA, xq_tok)]
            prev = ctx.guard.get("f1gu")
            if prev:
                mw.append(prev)
            for dc in range(DC):
                p.emit("tensor", lambda e, o=pg[:, 0:w], l=ctx.xnf(dc)[:, tsl],
                       r=wg_sb[:, dc * 512:dc * 512 + w], st=(dc == 0), sp=(dc == DC - 1):
                       e.matmul(out=o, lhsT=l, rhs=r, start=st, stop=sp),
                       waits=mw if dc == 0 else [])
                p.emit("tensor", lambda e, o=pu[:, 0:w], l=ctx.xnf(dc)[:, tsl],
                       r=wu_sb[:, dc * 512:dc * 512 + w], st=(dc == 0), sp=(dc == DC - 1):
                       e.matmul(out=o, lhsT=l, rhs=r, start=st, stop=sp),
                       inc=(p.sT, 1) if dc == DC - 1 else None)
            gu_mm = p.sT.n
            if tg == 15:
                ctx.guard[("f1g", sl)] = (p.sT, gu_mm)
            sg = hb[:, 4096:4608]
            htm = hb[:, 4608:5120]
            sw = [(p.sT, gu_mm)]
            prev = ctx.guard.get("f1sg")
            if prev:
                sw.append(prev)
            p.emit("scalar", lambda e, o=sg[:, 0:w], a=pg[:, 0:w]:
                   e.activation(out=o, in_=a, func=AF.Silu), waits=sw, inc=(p.sA, 1))
            sg_tok = p.sA.n
            p.emit("vector", lambda e, o=htm[:, 0:w], a=sg[:, 0:w], b=pu[:, 0:w]:
                   e.tensor_mul(out=o, in0=a, in1=b),
                   waits=[(p.sA, sg_tok), (p.sV, p.sV.n)], inc=(p.sV, 1))
            ctx.guard["f1gu"] = (p.sV, p.sV.n)
            ctx.guard["f1sg"] = (p.sV, p.sV.n)
            hq_st = ctx.wbig[:, 30720 + (tg % 2) * 512: 30720 + (tg % 2) * 512 + 512]
            prev_st = ctx.guard.get(("f1hst", tg % 2), 0)
            if prev_st:
                p.wait("vector", p.sX, prev_st)
            emit_quant(ctx, hq_st[:, 0:w], htm[:, 0:w], w, 1.0 / 6.0)
            stok = p.dma("sync", wd["hq_dram"][tsl, fs:fe], hq_st[:, 0:w],
                         waits=[(p.sV, p.sV.n)])
            ctx.guard[("f1hst", tg % 2)] = stok
    hq_all = p.sX.n
    if stage < 3:
        return

    for tc in range(4):
        csl = slice(tc * 512, (tc + 1) * 512)
        lw = [(p.sX, hq_all)]
        prev = ctx.guard.get("f1hqT")
        if prev:
            lw.append(prev)
        for fc in range(22):
            p.dma("sync", ctx.hbig[:, fc * 512:(fc + 1) * 512],
                  wd["hq_dram"][csl, fc * 128:(fc + 1) * 128],
                  waits=lw if fc == 0 else (), transpose=True)
        hqT_tok = p.sX.n
        for dc in range(DC):
            pdw = ctx.pb[pdwb + dc % 2]
            dwx = [(p.sX, hqT_tok), (p.sX, wdq_tok)]
            prev = ctx.guard.get(("f1dw", dc % 2))
            if prev:
                dwx.append(prev)
            for fc in range(22):
                p.emit("tensor", lambda e, o=pdw[:],
                       l=ctx.wbig[:, fc * 1024 + dc * 128: fc * 1024 + dc * 128 + 128],
                       r=ctx.hbig[:, fc * 512:(fc + 1) * 512],
                       st=(fc == 0), sp=(fc == 21):
                       e.matmul(out=o, lhsT=l, rhs=r, start=st, stop=sp),
                       waits=dwx if fc == 0 else [],
                       inc=(p.sT, 1) if fc == 21 else None)
            dmm = p.sT.n
            if dc == DC - 1:
                ctx.guard["f1hqT"] = (p.sT, dmm)
            p.emit("vector", lambda e, o=ctx.x(dc)[:, csl], a=ctx.x(dc)[:, csl], b=pdw[:]:
                   e.tensor_add(out=o, in0=a, in1=b),
                   waits=[(p.sT, dmm), (p.sV, p.sV.n)], inc=(p.sV, 1))
            ctx.guard[("f1dw", dc % 2)] = (p.sV, p.sV.n)
            ctx.x_tok[dc] = p.sV.n
    ctx.guard["wdq_free"] = ctx.guard["f1hqT"]
    ctx.guard["hbig_free"] = p.sT.n
    ctx.guard["xn_free"] = p.sT.n
    ctx.guard["gu_done"] = 0
    ctx.guard["down_done"] = p.sT.n


def declare_ffn1_weights(nc, with_prep):
    wd = {}
    if with_prep:
        wd["w_gate"] = nc.dram_tensor("w_gate", [F_L1, D], F32, kind="ExternalInput").ap()
        wd["w_up"] = nc.dram_tensor("w_up", [F_L1, D], F32, kind="ExternalInput").ap()
        wd["w_down"] = nc.dram_tensor("w_down", [D, F_L1], F32, kind="ExternalInput").ap()
    kind = "Internal" if with_prep else "ExternalInput"
    wd["wgq_nat"] = nc.dram_tensor("wgq_nat", [F_L1, D], BF16, kind=kind).ap()
    wd["wuq_nat"] = nc.dram_tensor("wuq_nat", [F_L1, D], BF16, kind=kind).ap()
    wd["wdq_nat"] = nc.dram_tensor("wdq_nat", [D, F_L1], BF16, kind=kind).ap()
    wd["hq_dram"] = nc.dram_tensor("hq_dram", [T, F_L1], BF16, kind="Internal").ap()
    return wd


def build_nc_ffn1():
    nc = bass.Bass("TRN2", target_bir_lowering=False, detect_race_conditions=False)
    p = Prog(nc)
    xin = nc.dram_tensor("xin", [D, T], F32, kind="ExternalInput").ap()
    xout = nc.dram_tensor("xout", [D, T], F32, kind="ExternalOutput").ap()
    wd = declare_ffn1_weights(nc, with_prep=True)
    ctx = Ctx(nc, p)
    emit_prologue(ctx, xin, need_moe=False)
    emit_wprep(ctx, wd)
    emit_ffn1(ctx, wd, "f1")
    emit_epilogue(ctx, xout)
    p.build()
    return nc


def ffn1_in_maps(x_full, inputs):
    ident, _ = _host_constants()
    maps = []
    for b in range(NCORE):
        maps.append({
            "xin": np.ascontiguousarray(np.asarray(x_full[b], np.float32).T),
            "ident": ident,
            "w_gate": np.asarray(inputs["L1_ffn_w_gate"], np.float32),
            "w_up": np.asarray(inputs["L1_ffn_w_up"], np.float32),
            "w_down": np.asarray(inputs["L1_ffn_w_down"], np.float32),
        })
    return maps


def run_ffn1_phase(x_full, inputs):
    nc = build_nc_ffn1()
    res = run_bass_kernel_spmd(nc, ffn1_in_maps(x_full, inputs), core_ids=list(range(NCORE)))
    return np.stack([np.ascontiguousarray(r["xout"].T) for r in res.results])


def emit_attn1(ctx, wd):
    p = ctx.p
    QF, KF = 0, 16384
    xn_tok = ctx._attn_xn_tok
    hb_guard = ctx.guard.get("hbig_free", 0)
    hbf = ctx.hbig[:].bitcast(F32)
    abf = ctx.arena[:].bitcast(BF16)

    wname = "w_qkvT"
    for fc in range(24):
        ws = ctx.hbig[:, 8192 + (fc % 2) * 1024: 8192 + (fc % 2) * 1024 + 1024]
        lw = []
        prev = ctx.guard.get(("a1w", fc % 2))
        if prev:
            lw.append(prev)
        elif hb_guard:
            lw.append((p.sT, hb_guard))
        wtok = p.dma("gpsimd", ws.rearrange("p (dc f) -> p dc f", f=128),
                     wd[wname].rearrange("(dc p) f -> p dc f", p=128)[:, :, fc * 128:(fc + 1) * 128],
                     waits=lw)
        for qc in range(4):
            ps = ctx.pb[qc % 2]
            w0 = [(p.sW, wtok), (p.sV, xn_tok)]
            prev = ctx.guard.get(("a1p", qc % 2))
            if prev:
                w0.append(prev)
            for dc in range(DC):
                p.emit("tensor", lambda e, o=ps[:], l=ws[:, dc * 128:(dc + 1) * 128],
                       r=ctx.xnf(dc)[:, qc * 512:(qc + 1) * 512],
                       st=(dc == 0), sp=(dc == DC - 1):
                       e.matmul(out=o, lhsT=l, rhs=r, start=st, stop=sp),
                       waits=w0 if dc == 0 else [],
                       inc=(p.sT, 1) if dc == DC - 1 else None)
            mm = p.sT.n
            if qc == 3:
                ctx.guard[("a1w", fc % 2)] = (p.sT, mm)
            if fc < 16:
                ooff = QF if fc < 8 else KF
                fl = fc % 8
                p.emit("scalar", lambda e, o=ctx.wbig[:, ooff + fl * 2048 + qc * 512: ooff + fl * 2048 + (qc + 1) * 512],
                       a=ps[:]:
                       e.copy(out=o, in_=a), waits=[(p.sT, mm)], inc=(p.sA, 1))
                ctx.guard[("a1p", qc % 2)] = (p.sA, p.sA.n)
            else:
                vs = ctx.hbig[:, 10240 + (qc % 2) * 512: 10240 + (qc % 2) * 512 + 512]
                vw = [(p.sT, mm)]
                prev_v = ctx.guard.get(("a1v", qc % 2), 0)
                if prev_v:
                    vw.append((p.sX, prev_v))
                p.emit("scalar", lambda e, o=vs, a=ps[:]:
                       e.copy(out=o, in_=a), waits=vw, inc=(p.sA, 1))
                ctx.guard[("a1p", qc % 2)] = (p.sA, p.sA.n)
                st = p.dma("sync", wd["v_dram"][(fc - 16) * 128:(fc - 15) * 128, qc * 512:(qc + 1) * 512],
                           vs, waits=[(p.sA, p.sA.n)])
                ctx.guard[("a1v", qc % 2)] = st
    qk_tok = p.sA.n
    v_all = p.sX.n

    tw = []
    if hb_guard:
        tw.append((p.sT, hb_guard))
    p.dma("sync", hbf[:, 0:2048], wd["cos_tbl"], waits=tw)
    p.dma("sync", hbf[:, 2048:4096], wd["sin_tbl"])
    tbl_tok = p.sX.n

    def q_head(h, sl):
        return ctx.wbig[(h % 2) * 64:(h % 2) * 64 + 64, QF + (h // 2) * 2048 + sl.start: QF + (h // 2) * 2048 + sl.stop]

    def k_head(h, sl):
        return ctx.wbig[(h % 2) * 64:(h % 2) * 64 + 64, KF + (h // 2) * 2048 + sl.start: KF + (h // 2) * 2048 + sl.stop]

    rotscr = ctx.rotscr
    for h in range(NH):
        pp = (h % 2) * 64
        hsl = slice(pp, pp + 64)
        vtm = ctx.hbig[:, 8192 + (h % 2) * 1024: 8192 + (h % 2) * 1024 + 1024]
        vw = [(p.sX, v_all)]
        prev = ctx.guard.get(("a1vt", h % 2))
        if prev:
            vw.append(prev)
        if h < 2:
            pw = ctx.guard.get(("a1w", h % 2))
            if pw:
                vw.append(pw)
        for kc in range(16):
            p.dma("sync", vtm[:, kc * 64:(kc + 1) * 64],
                  wd["v_dram"][h * 64:(h + 1) * 64, kc * 128:(kc + 1) * 128],
                  waits=vw if kc == 0 else (), transpose=True)
        vtm_tok = p.sX.n

        qr = abf[hsl, 0:2048]
        kr = abf[hsl, 2048:4096]
        rot_jobs = [(q_head, qr), (k_head, kr)]
        for ji, (src_head, dstb) in enumerate(rot_jobs):
            for sc in range(4):
                ssl = slice(sc * 512, (sc + 1) * 512)
                pperm = ctx.pb[6]
                mw = [(p.sA, qk_tok), (p.sX, tbl_tok)]
                prev = ctx.guard.get("a1perm")
                if prev:
                    mw.append(prev)
                p.emit("tensor", lambda e, o=pperm[0:64, :], l=ctx.pswap[hsl, :],
                       r=src_head(h, ssl):
                       e.matmul(out=o, lhsT=l, rhs=r, start=True, stop=True),
                       waits=mw, inc=(p.sT, 1))
                pm_tok = p.sT.n
                p.emit("vector", lambda e, o=rotscr[hsl, 0:512], a=src_head(h, ssl),
                       b=hbf[hsl, ssl]:
                       e.tensor_mul(out=o, in0=a, in1=b),
                       waits=[(p.sV, p.sV.n), (p.sX, tbl_tok), (p.sA, qk_tok)],
                       inc=(p.sV, 1))
                p.emit("vector", lambda e, o=rotscr[hsl, 512:1024], a=ctx.pb[6][0:64, :],
                       b=hbf[hsl, 2048 + sc * 512:2048 + (sc + 1) * 512]:
                       e.tensor_mul(out=o, in0=a, in1=b),
                       waits=[(p.sT, pm_tok), (p.sV, p.sV.n)], inc=(p.sV, 1))
                rtw = [(p.sV, p.sV.n)]
                pqr = ctx.guard.get(("a1qr", h % 2))
                if pqr:
                    rtw.append(pqr)
                p.emit("vector", lambda e, o=dstb[:, ssl], a=rotscr[hsl, 0:512],
                       b=rotscr[hsl, 512:1024]:
                       e.tensor_add(out=o, in0=a, in1=b),
                       waits=rtw, inc=(p.sV, 1))
                ctx.guard["a1perm"] = (p.sV, p.sV.n)
        rot_tok = p.sV.n

        den = ctx.sm32[:, 384:400]
        for qg in range(16):
            pd = ctx.pb[qg % 2]
            dumps = ctx.hbig[:, 10240 + (qg % 2) * 512: 10240 + (qg % 2) * 512 + 512]
            parts = ctx.sm32[:, 416 + (qg % 2) * 4: 420 + (qg % 2) * 4]
            for kc in range(4):
                mw = [(p.sA, qk_tok)]
                prev = ctx.guard.get(("a1d", qg % 2))
                if prev:
                    mw.append(prev)
                p.emit("tensor", lambda e, o=pd[:], l=q_head(h, slice(qg * 128, (qg + 1) * 128)),
                       r=k_head(h, slice(kc * 512, (kc + 1) * 512)):
                       e.matmul(out=o, lhsT=l, rhs=r, start=True, stop=True),
                       waits=mw, inc=(p.sT, 1))
                dmm = p.sT.n
                dpw = [(p.sT, dmm)]
                if h == 0 and qg == 0 and kc == 0:
                    dpw.append((p.sX, v_all))
                p.emit("scalar", lambda e, o=dumps, a=pd[:], acc=parts[:, kc:kc + 1]:
                       e.activation(out=o, in_=a, func=AF.Exp, scale=0.125, accum_out=acc),
                       waits=dpw, inc=(p.sA, 1))
                ctx.guard[("a1d", qg % 2)] = (p.sA, p.sA.n)
            de_tok = p.sA.n
            p.emit("vector", lambda e, o=den[:, qg:qg + 1], i=parts:
                   e.reduce_sum(out=o, in_=i, axis=AX),
                   waits=[(p.sA, de_tok), (p.sV, p.sV.n)], inc=(p.sV, 1))
        p.emit("vector", lambda e, o=ctx.sm32[:, 400:416], i=den:
               e.reciprocal(out=o, in_=i), waits=[(p.sV, p.sV.n)], inc=(p.sV, 1))
        rd_tok = p.sV.n
        trw = [(p.sV, rd_tok)]
        prev = ctx.guard.get("a1dtr")
        if prev:
            trw.append(prev)
        p.emit("tensor", lambda e, o=ctx.pb[7][0:16, 0:128], i=ctx.sm32[:, 400:416],
               idt=ctx.ident_f32[:]:
               e.transpose(out=o, in_=i, identity=idt), waits=trw, inc=(p.sT, 1))
        dtr_tok = p.sT.n
        cw2 = [(p.sT, dtr_tok)]
        prev = ctx.guard.get("a1dcp")
        if prev:
            cw2.append(prev)
        p.emit("scalar", lambda e, o=ctx.smbf[0:16, 320:448], a=ctx.pb[7][0:16, 0:128]:
               e.copy(out=o, in_=a), waits=cw2, inc=(p.sA, 1))
        ctx.guard["a1dtr"] = (p.sA, p.sA.n)
        dd = p.dma("sync", wd["den_dram"][h].rearrange("(g q) -> g q", q=128),
                   ctx.smbf[0:16, 320:448], waits=[(p.sA, p.sA.n)])
        ctx.guard["a1dcp"] = (p.sX, dd)

        for qc in range(4):
            qsl = slice(qc * 512, (qc + 1) * 512)
            ld = p.dma("sync", ctx.rdbf[0:1, :], wd["den_dram"][h:h + 1, qsl],
                       waits=[(p.sX, dd), (p.sT, ctx.guard.get("a1rbm", (None, 0))[1])])
            prev = ctx.guard.get("a1rb")
            rw = [(p.sX, ld)]
            if prev:
                rw.append(prev)
            p.emit("tensor", lambda e, o=ctx.pb[5][:], l=ctx.smbf[0:1, 0:128], r=ctx.rdbf[0:1, :]:
                   e.matmul(out=o, lhsT=l, rhs=r, start=True, stop=True),
                   waits=rw, inc=(p.sT, 1))
            rbm_tok = p.sT.n
            ctx.guard["a1rbm"] = (p.sT, rbm_tok)
            p.emit("scalar", lambda e, o=ctx.smbf[:, 512:1024], a=ctx.pb[5][:]:
                   e.copy(out=o, in_=a), waits=[(p.sT, rbm_tok)], inc=(p.sA, 1))
            rdbc_tok = p.sA.n
            ctx.guard["a1rb"] = (p.sA, rdbc_tok)

            po = ctx.pb[4]
            nkc = qc * 4 + 4
            for kc in range(nkc):
                pat = ctx.pb[kc % 2]
                pex = ctx.pb[2 + kc % 2]
                mw = [(p.sV, rot_tok)]
                prev = ctx.guard.get(("a1n", kc % 2))
                if prev:
                    mw.append(prev)
                p.emit("tensor", lambda e, o=pat[:], l=kr[:, kc * 128:(kc + 1) * 128],
                       r=qr[:, qsl]:
                       e.matmul(out=o, lhsT=l, rhs=r, start=True, stop=True),
                       waits=mw, inc=(p.sT, 1))
                at_tok = p.sT.n
                ew = [(p.sT, at_tok)]
                prev = ctx.guard.get(("a1e", kc % 2))
                if prev:
                    ew.append(prev)
                p.emit("scalar", lambda e, o=pex[:], a=pat[:]:
                       e.activation(out=o, in_=a, func=AF.Exp, scale=0.125),
                       waits=ew, inc=(p.sA, 1))
                ex_tok = p.sA.n
                ctx.guard[("a1n", kc % 2)] = (p.sA, ex_tok)
                pt = ctx.hbig[:, 10240 + (kc % 2) * 512: 10240 + (kc % 2) * 512 + 512]
                ptw = [(p.sA, ex_tok), (p.sA, rdbc_tok), (p.sV, p.sV.n)]
                prev = ctx.guard.get(("a1pt", kc % 2))
                if prev:
                    ptw.append(prev)
                diag = (kc >= qc * 4)
                if diag:
                    off = 384 - (kc * 128 - qc * 512)
                    p.emit("vector", lambda e, o=pt, a=pex[:], b=ctx.smbf[:, 512:1024]:
                           e.tensor_mul(out=o, in0=a, in1=b), waits=ptw, inc=(p.sV, 1))
                    p.emit("vector", lambda e, o=pt, a=pt, b=ctx.trib[:, off:off + 512]:
                           e.tensor_mul(out=o, in0=a, in1=b),
                           waits=[(p.sV, p.sV.n)], inc=(p.sV, 1))
                else:
                    p.emit("vector", lambda e, o=pt, a=pex[:], b=ctx.smbf[:, 512:1024]:
                           e.tensor_mul(out=o, in0=a, in1=b), waits=ptw, inc=(p.sV, 1))
                pt_tok = p.sV.n
                ctx.guard[("a1e", kc % 2)] = (p.sV, pt_tok)
                vw2 = [(p.sV, pt_tok), (p.sX, vtm_tok)]
                if kc == 0:
                    prev = ctx.guard.get("a1o")
                    if prev:
                        vw2.append(prev)
                p.emit("tensor", lambda e, o=po[0:64, :], l=vtm[:, kc * 64:(kc + 1) * 64],
                       r=pt, st=(kc == 0), sp=(kc == nkc - 1):
                       e.matmul(out=o, lhsT=l, rhs=r, start=st, stop=sp),
                       waits=vw2, inc=(p.sT, 1))
                ctx.guard[("a1pt", kc % 2)] = (p.sT, p.sT.n)
            o_tok = p.sT.n
            dc = h // 2
            ysl = slice(dc * 2048 + qc * 512, dc * 2048 + (qc + 1) * 512)
            if pp == 0:
                yw = [(p.sT, o_tok)]
                prev = ctx.guard.get("a1y")
                if prev:
                    yw.append(prev)
                p.emit("scalar", lambda e, o=ctx.xnbig[0:64, ysl], a=po[0:64, :]:
                       e.copy(out=o, in_=a), waits=yw, inc=(p.sA, 1))
                ctx.guard["a1y"] = (p.sA, p.sA.n)
                ctx.guard["a1o"] = (p.sA, p.sA.n)
            else:
                p.emit("scalar", lambda e, o=ctx.smbf[0:64, 512:1024], a=po[0:64, :]:
                       e.copy(out=o, in_=a),
                       waits=[(p.sT, o_tok), (p.sA, rdbc_tok)], inc=(p.sA, 1))
                ybf_tok = p.sA.n
                ctx.guard["a1rb"] = (p.sA, ybf_tok)
                prev = ctx.guard.get("a1rl")
                rl = [(p.sA, ybf_tok)]
                if prev:
                    rl.append(prev)
                p.emit("tensor", lambda e, o=ctx.pb[6][64:128, :], l=ctx.smbf[0:64, 128:192],
                       r=ctx.smbf[0:64, 512:1024]:
                       e.matmul(out=o, lhsT=l, rhs=r, start=True, stop=True,
                                tile_position=(0, 64)),
                       waits=rl, inc=(p.sT, 1))
                rl_tok = p.sT.n
                p.emit("scalar", lambda e, o=ctx.xnbig[64:128, ysl], a=ctx.pb[6][64:128, :]:
                       e.copy(out=o, in_=a), waits=[(p.sT, rl_tok)], inc=(p.sA, 1))
                ctx.guard["a1rl"] = (p.sA, p.sA.n)
                ctx.guard["a1o"] = (p.sA, p.sA.n)
                ctx.guard["a1perm"] = (p.sA, p.sA.n)
        ctx.guard[("a1vt", h % 2)] = (p.sT, p.sT.n)
        ctx.guard[("a1qr", h % 2)] = (p.sT, p.sT.n)
    y_tok = p.sA.n

    ow = []
    prev = ctx.guard.get("a1qkfree")
    p.dma("gpsimd", ctx.wbig[:, 0:8192].rearrange("p (dc f) -> p dc f", f=1024),
          wd["w_outT"].rearrange("(dc p) f -> p dc f", p=128),
          waits=[(p.sT, p.sT.n)])
    wo_tok = p.sW.n
    p.dma("sync", ctx.sm32[:, 16:24], wd["bout_sb"])
    bo_tok = p.sX.n
    for fc in range(8):
        for qc in range(4):
            ps = ctx.pb[qc % 2]
            w0 = [(p.sW, wo_tok), (p.sA, y_tok)]
            prev = ctx.guard.get(("a1op", qc % 2))
            if prev:
                w0.append(prev)
            for dc in range(DC):
                p.emit("tensor", lambda e, o=ps[:],
                       l=ctx.wbig[:, dc * 1024 + fc * 128: dc * 1024 + fc * 128 + 128],
                       r=ctx.xnbig[:, dc * 2048 + qc * 512: dc * 2048 + (qc + 1) * 512],
                       st=(dc == 0), sp=(dc == DC - 1):
                       e.matmul(out=o, lhsT=l, rhs=r, start=st, stop=sp),
                       waits=w0 if dc == 0 else [],
                       inc=(p.sT, 1) if dc == DC - 1 else None)
            mm = p.sT.n
            p.emit("vector", lambda e, o=ctx.x(fc)[:, qc * 512:(qc + 1) * 512],
                   a=ps[:], b=ctx.sm32[:, 16 + fc:17 + fc],
                   c=ctx.x(fc)[:, qc * 512:(qc + 1) * 512]:
                   e.scalar_tensor_tensor(out=o, in0=a, scalar=b, in1=c,
                                          op0=ALU.add, op1=ALU.add),
                   waits=[(p.sT, mm), (p.sX, bo_tok), (p.sV, p.sV.n)], inc=(p.sV, 1))
            ctx.guard[("a1op", qc % 2)] = (p.sV, p.sV.n)
            ctx.x_tok[fc] = p.sV.n
    ctx.guard["hbig_free"] = p.sT.n
    ctx.guard["xn_free"] = p.sT.n
    ctx.guard["gu_done"] = 0
    ctx.guard["down_done"] = p.sT.n


def emit_attn1_sublayer(ctx, wd):
    emit_ln(ctx, 0, full=True)
    t1 = emit_ln(ctx, 1, full=True)
    ctx._attn_xn_tok = t1
    emit_attn1(ctx, wd)


def declare_attn1_weights(nc):
    wd = {}
    wd["w_qkvT"] = nc.dram_tensor("w_qkvT", [D, 3 * D], F32, kind="ExternalInput").ap()
    wd["w_outT"] = nc.dram_tensor("w_outT", [D, D], F32, kind="ExternalInput").ap()
    wd["bout_sb"] = nc.dram_tensor("bout_sb", [128, 8], F32, kind="ExternalInput").ap()
    wd["cos_tbl"] = nc.dram_tensor("cos_tbl", [128, T], F32, kind="ExternalInput").ap()
    wd["sin_tbl"] = nc.dram_tensor("sin_tbl", [128, T], F32, kind="ExternalInput").ap()
    wd["pswap_t"] = nc.dram_tensor("pswap_t", [128, 64], F32, kind="ExternalInput").ap()
    wd["trib_t"] = nc.dram_tensor("trib_t", [128, 896], F32, kind="ExternalInput").ap()
    wd["v_dram"] = nc.dram_tensor("v_dram", [D, T], BF16, kind="Internal").ap()
    wd["den_dram"] = nc.dram_tensor("den_dram", [NH, T], BF16, kind="Internal").ap()
    return wd


def emit_attn1_consts(ctx, wd):
    p = ctx.p
    p.dma("gpsimd", ctx.pswap[:], wd["pswap_t"])
    p.dma("gpsimd", ctx.trib[:], wd["trib_t"])
    ctx.guard["a1c"] = p.sW.n


def build_nc_attn1():
    nc = bass.Bass("TRN2", target_bir_lowering=False, detect_race_conditions=False)
    p = Prog(nc)
    xin = nc.dram_tensor("xin", [D, T], F32, kind="ExternalInput").ap()
    xout = nc.dram_tensor("xout", [D, T], F32, kind="ExternalOutput").ap()
    wd = declare_attn1_weights(nc)
    ctx = Ctx(nc, p)
    ctx.rotscr = ctx.arena[:, 2048:3072]
    ctx.rdbf = nc.alloc_sbuf_tensor("rdbf", [1, 512], BF16)
    ctx.pswap = nc.alloc_sbuf_tensor("pswap", [128, 64], BF16)
    ctx.trib = nc.alloc_sbuf_tensor("trib", [128, 896], BF16)
    emit_prologue(ctx, xin, need_moe=False)
    emit_attn1_consts(ctx, wd)
    emit_attn1_sublayer(ctx, wd)
    emit_epilogue(ctx, xout)
    p.build()
    return nc


def _attn1_consts():
    i = np.arange(32, dtype=np.float64)
    theta = 10000.0 ** (-2.0 * (i - 1.0) / 64.0)
    t = np.arange(T, dtype=np.float64)
    ang = t[None, :] * theta[:, None]
    cos32 = np.cos(ang)
    sin32 = np.sin(ang)
    cos64 = np.repeat(cos32, 2, axis=0)
    sin64 = np.repeat(sin32, 2, axis=0)
    sin64[1::2] *= -1.0
    cos128 = np.tile(cos64, (2, 1)).astype(np.float32)
    sin128 = np.tile(sin64, (2, 1)).astype(np.float32)
    P = np.zeros((64, 64), np.float32)
    for m in range(32):
        P[2 * m + 1, 2 * m] = 1.0
        P[2 * m, 2 * m + 1] = 1.0
    pswap = np.tile(P, (2, 1)).astype(np.float32)
    trib = np.zeros((128, 896), np.float32)
    for ii in range(128):
        for jj in range(896):
            if (jj - 384) >= ii:
                trib[ii, jj] = 1.0
    return cos128, sin128, pswap, trib


def attn1_in_maps(x_full, inputs):
    ident, _ = _host_constants()
    cos128, sin128, pswap, trib = _attn1_consts()
    bout = np.asarray(inputs["L1_attn_b_out"], np.float32).reshape(8, 128).T.copy()
    wqkvT = np.ascontiguousarray(np.asarray(inputs["L1_attn_w_qkv"], np.float32).T)
    woutT = np.ascontiguousarray(np.asarray(inputs["L1_attn_w_out"], np.float32).T)
    maps = []
    for b in range(NCORE):
        maps.append({
            "xin": np.ascontiguousarray(np.asarray(x_full[b], np.float32).T),
            "ident": ident,
            "w_qkvT": wqkvT,
            "w_outT": woutT,
            "bout_sb": np.ascontiguousarray(bout),
            "cos_tbl": cos128,
            "sin_tbl": sin128,
            "pswap_t": pswap,
            "trib_t": trib,
        })
    return maps


def run_attn1_phase(x_full, inputs):
    nc = build_nc_attn1()
    res = run_bass_kernel_spmd(nc, attn1_in_maps(x_full, inputs), core_ids=list(range(NCORE)))
    return np.stack([np.ascontiguousarray(r["xout"].T) for r in res.results])


ALL_PHASES = ("moe", "attn0", "moe", "ffn1", "attn1", "ffn1")


def build_nc_span(phases):
    nc = bass.Bass("TRN2", target_bir_lowering=False, detect_race_conditions=False)
    p = Prog(nc)
    xin = nc.dram_tensor("xin", [D, T], F32, kind="ExternalInput").ap()
    xout = nc.dram_tensor("xout", [D, T], F32, kind="ExternalOutput").ap()
    need_moe = "moe" in phases
    n_ffn = sum(1 for ph in phases if ph.startswith("ffn1"))
    wd = {}
    if need_moe:
        wd.update(declare_moe_weights(nc))
    if "attn0" in phases:
        wd.update(declare_attn0_weights(nc))
    if n_ffn:
        wd.update(declare_ffn1_weights(nc, with_prep=True))
        if n_ffn > 1:
            wd["hq_dram_b"] = nc.dram_tensor("hq_dram_b", [T, F_L1], BF16, kind="Internal").ap()
    if "attn1" in phases:
        wd.update(declare_attn1_weights(nc))
    ctx = Ctx(nc, p)
    if "attn1" in phases:
        ctx.rotscr = ctx.arena[:, 2048:3072]
        ctx.rdbf = nc.alloc_sbuf_tensor("rdbf", [1, 512], BF16)
        ctx.pswap = nc.alloc_sbuf_tensor("pswap", [128, 64], BF16)
        ctx.trib = nc.alloc_sbuf_tensor("trib", [128, 896], BF16)
    emit_prologue(ctx, xin, need_moe=need_moe)
    if "attn1" in phases:
        emit_attn1_consts(ctx, wd)
    ffn_i = 0
    seen_attn1 = False
    for ph in phases:
        dd = ctx.guard.get("down_done", 0)
        if ph == "moe":
            ctx.guard[("gu_done", 0)] = dd
            ctx.guard[("gu_done", 1)] = dd
            emit_moe_sublayer(ctx, wd)
        elif ph == "attn0":
            emit_attn0_sublayer(ctx, wd)
        elif ph.startswith("ffn1"):
            stage = 99 if ph == "ffn1" else int(ph[5:])
            if ffn_i == 0:
                if dd:
                    ctx.guard[("wp", 0)] = (p.sT, dd)
                    ctx.guard[("wp", 1)] = (p.sT, dd)
                emit_wprep(ctx, wd)
            if dd:
                ctx.guard[("f1g", 0)] = (p.sT, dd)
                ctx.guard[("f1g", 1)] = (p.sT, dd)
                ctx.guard["wdq_free"] = (p.sT, dd)
            wdx = dict(wd)
            if ffn_i == 1:
                wdx["hq_dram"] = wd["hq_dram_b"]
            emit_ffn1(ctx, wdx, "f1_%d" % ffn_i, stage=stage,
                      banks=(4, 4, 2) if seen_attn1 else (0, 2, 6))
            ffn_i += 1
        elif ph == "attn1":
            emit_attn1_sublayer(ctx, wd)
            seen_attn1 = True
    emit_epilogue(ctx, xout)
    p.build()
    return nc


def span_in_maps(x_full, inputs, phases):
    ident, onehot = _host_constants()
    common = {"ident": ident}
    if "moe" in phases:
        common.update({
            "gate_w_sb": _rearrange_gate_w(np.asarray(inputs["L0_ffn_gate_w"], np.float32)),
            "onehot_bf": onehot,
            "exp_gate": np.asarray(inputs["L0_ffn_exp_gate"], np.float32),
            "exp_up": np.asarray(inputs["L0_ffn_exp_up"], np.float32),
            "exp_down": np.asarray(inputs["L0_ffn_exp_down"], np.float32),
            "sh_gate": np.asarray(inputs["L0_ffn_sh_gate"], np.float32),
            "sh_up": np.asarray(inputs["L0_ffn_sh_up"], np.float32),
            "sh_down": np.asarray(inputs["L0_ffn_sh_down"], np.float32),
        })
    if "attn0" in phases:
        common.update({
            "Wq": np.asarray(inputs["L0_attn_Wq"], np.float32),
            "Wk": np.asarray(inputs["L0_attn_Wk"], np.float32),
            "bq_sb": np.ascontiguousarray(np.asarray(inputs["L0_attn_bq"], np.float32).reshape(8, 128).T),
            "bk_sb": np.ascontiguousarray(np.asarray(inputs["L0_attn_bk"], np.float32).reshape(8, 128).T),
        })
    if any(ph.startswith("ffn1") for ph in phases):
        common.update({
            "w_gate": np.asarray(inputs["L1_ffn_w_gate"], np.float32),
            "w_up": np.asarray(inputs["L1_ffn_w_up"], np.float32),
            "w_down": np.asarray(inputs["L1_ffn_w_down"], np.float32),
        })
    if "attn1" in phases:
        cos128, sin128, pswap, trib = _attn1_consts()
        common.update({
            "w_qkvT": np.ascontiguousarray(np.asarray(inputs["L1_attn_w_qkv"], np.float32).T),
            "w_outT": np.ascontiguousarray(np.asarray(inputs["L1_attn_w_out"], np.float32).T),
            "bout_sb": np.ascontiguousarray(np.asarray(inputs["L1_attn_b_out"], np.float32).reshape(8, 128).T),
            "cos_tbl": cos128,
            "sin_tbl": sin128,
            "pswap_t": pswap,
            "trib_t": trib,
        })
    maps = []
    for b in range(NCORE):
        m = dict(common)
        m["xin"] = np.ascontiguousarray(np.asarray(x_full[b], np.float32).T)
        maps.append(m)
    return maps


def run_span(x_full, inputs, phases):
    nc = build_nc_span(phases)
    res = run_bass_kernel_spmd(nc, span_in_maps(x_full, inputs, phases),
                               core_ids=list(range(NCORE)))
    return np.stack([np.ascontiguousarray(r["xout"].T) for r in res.results])


def kernel(**inputs):
    x = np.asarray(inputs["L0_attn_x"], np.float32)
    return run_span(x, inputs, ALL_PHASES)


# version 05
# speedup vs baseline: 1.0260x; 1.0260x over previous
import numpy as np
import ml_dtypes
import concourse.bass as bass
import concourse.mybir as mybir
from concourse.bass_utils import run_bass_kernel_spmd

F32 = mybir.dt.float32
BF16 = mybir.dt.bfloat16
AF = mybir.ActivationFunctionType
ALU = mybir.AluOpType
AX = mybir.AxisListType.X

NCORE = 8
T = 2048
TH = 1024
D = 1024
DC = 8
NH = 16
DHD = 64
F_MOE = 1408
F_L1 = 2816
NEXP = 8
NSH = 2
BF = ml_dtypes.bfloat16

ENGINES = ("sync", "gpsimd", "vector", "scalar", "tensor")


class Sem:
    def __init__(self, nc, name):
        self.h = nc.alloc_semaphore(name)
        self.n = 0

    def bump(self, k=1):
        self.n += k
        return self.n


class Prog:
    def __init__(self, nc):
        self.nc = nc
        self.ops = {e: [] for e in ENGINES}
        self.sW = Sem(nc, "sW")
        self.sX = Sem(nc, "sX")
        self.sT = Sem(nc, "sT")
        self.sV = Sem(nc, "sV")
        self.sA = Sem(nc, "sA")
        self.sP = Sem(nc, "sP")

    def emit(self, eng, fn, waits=(), inc=None):
        waits = tuple((s.h, v) for (s, v) in waits if v > 0)
        inc_h = None
        if inc is not None:
            inc[0].bump(inc[1])
            inc_h = (inc[0].h, inc[1])

        def run(e, fn=fn, waits=waits, inc_h=inc_h):
            for h, v in waits:
                e.wait_ge(h, v)
            ins = fn(e)
            if inc_h is not None:
                ins.then_inc(inc_h[0], inc_h[1])
        self.ops[eng].append(run)

    def wait(self, eng, sem, val):
        if val > 0:
            self.emit(eng, lambda e: None, waits=[(sem, val)])

    def dma(self, queue, out, in_, waits=(), transpose=False):
        sem = self.sW if queue == "gpsimd" else self.sX
        self.emit(queue, lambda e, o=out, i=in_, t=transpose: e.dma_start(out=o, in_=i, transpose=t),
                  waits=waits, inc=(sem, 16))
        return sem.n

    def build(self):
        nc = self.nc
        with nc.Block() as block:
            @block.sync
            def _(e):
                for f in self.ops["sync"]:
                    f(e)

            @block.gpsimd
            def _(e):
                for f in self.ops["gpsimd"]:
                    f(e)

            @block.vector
            def _(e):
                for f in self.ops["vector"]:
                    f(e)

            @block.scalar
            def _(e):
                for f in self.ops["scalar"]:
                    f(e)

            @block.tensor
            def _(e):
                for f in self.ops["tensor"]:
                    f(e)
        return nc


class Ctx:
    def __init__(self, nc, p: Prog):
        self.nc = nc
        self.p = p
        self.xbig = nc.alloc_sbuf_tensor("xbig", [128, DC * T], F32)
        self.xnbig = nc.alloc_sbuf_tensor("xnbig", [128, DC * T], BF16)
        self.wbig = nc.alloc_sbuf_tensor("wbig", [128, 33792], BF16)
        self.hbig = nc.alloc_sbuf_tensor("hbig", [128, 11264], BF16)
        self.arena = nc.alloc_sbuf_tensor("arena", [128, 3072], F32)
        self.sm32 = nc.alloc_sbuf_tensor("sm32", [128, 512], F32)
        self.smbf = nc.alloc_sbuf_tensor("smbf", [128, 1024], BF16)
        self.onehot = nc.alloc_sbuf_tensor("onehot", [8, 1024], BF16)
        self.wte_em = nc.alloc_sbuf_tensor("wte_em", [8, 1024], BF16)
        self.ident_f32 = nc.alloc_sbuf_tensor("ident_f32", [128, 128], F32)
        self.pb = [nc.alloc_psum_tensor(f"pb{i}", [128, 512], F32) for i in range(8)]
        self.x_tok = [0] * DC
        self.guard = {}

    def x(self, dc):
        return self.xbig[:, dc * T:(dc + 1) * T]

    def xnh(self, th, dc):
        return self.xnbig[:, th * 8192 + dc * TH: th * 8192 + (dc + 1) * TH]

    def xnf(self, dc):
        return self.xnbig[:, dc * T:(dc + 1) * T]


def emit_ln(ctx, th, full=False):
    p = ctx.p
    m_bc = ctx.arena[:, 0:TH]
    r_bc = ctx.arena[:, TH:2 * TH]
    ones_bf = ctx.smbf[:, 0:128]

    def xc_slot(i):
        return ctx.hbig[:, (i % 4) * 512:(i % 4) * 512 + 512]

    def sq_slot(i):
        return ctx.hbig[:, 2048 + (i % 4) * 512:2048 + (i % 4) * 512 + 512]

    mm_tok = {}
    hb_guard = ctx.guard.get("hbig_free", 0)
    base = ctx.guard.get("consts_ready", 0)
    prev_evac = 0

    for tc in range(2):
        cs = slice(th * TH + tc * 512, th * TH + (tc + 1) * 512)
        hs = slice(tc * 512, (tc + 1) * 512)
        pm, pv = ctx.pb[0], ctx.pb[1]
        for dc in range(DC):
            i = tc * 8 + dc
            first = (i == 0)
            wv = [(p.sT, mm_tok[i - 4])] if i - 4 in mm_tok else []
            wa = list(wv)
            if first:
                extra = [(p.sV, base), (p.sT, hb_guard)] + \
                        [(p.sV, ctx.x_tok[dcx]) for dcx in range(DC)]
                wv += extra
                wa += extra
            p.emit("vector", lambda e, o=xc_slot(i), a=ctx.x(dc)[:, cs]:
                   e.tensor_copy(out=o, in_=a), waits=wv, inc=(p.sV, 1))
            c_tok = p.sV.n
            p.emit("scalar", lambda e, o=sq_slot(i), a=ctx.x(dc)[:, cs]:
                   e.activation(out=o, in_=a, func=AF.Square), waits=wa, inc=(p.sA, 1))
            s_tok = p.sA.n
            mw = [(p.sV, c_tok)]
            if dc == 0 and prev_evac:
                mw.append((p.sA, prev_evac))
            p.emit("tensor", lambda e, o=pm[:], l=ones_bf, r=xc_slot(i),
                   st=(dc == 0), sp=(dc == DC - 1):
                   e.matmul(out=o, lhsT=l, rhs=r, start=st, stop=sp), waits=mw)
            p.emit("tensor", lambda e, o=pv[:], l=ones_bf, r=sq_slot(i),
                   st=(dc == 0), sp=(dc == DC - 1):
                   e.matmul(out=o, lhsT=l, rhs=r, start=st, stop=sp),
                   waits=[(p.sA, s_tok)], inc=(p.sT, 1))
            mm_tok[i] = p.sT.n
        t_tok = p.sT.n
        p.emit("scalar", lambda e, o=m_bc[:, hs], a=pm[:]:
               e.mul(out=o, in_=a, mul=1.0 / D), waits=[(p.sT, t_tok)], inc=(p.sA, 1))
        p.emit("scalar", lambda e, o=r_bc[:, hs], a=pv[:]:
               e.mul(out=o, in_=a, mul=1.0 / D), inc=(p.sA, 1))
        a_tok = p.sA.n
        prev_evac = a_tok
        scr = ctx.arena[:, 2048 + tc * 512: 2048 + (tc + 1) * 512]
        p.emit("vector", lambda e, o=scr, a=m_bc[:, hs], b=m_bc[:, hs]:
               e.tensor_mul(out=o, in0=a, in1=b), waits=[(p.sA, a_tok)], inc=(p.sV, 1))
        p.emit("vector", lambda e, o=r_bc[:, hs], a=r_bc[:, hs], b=scr:
               e.tensor_sub(out=o, in0=a, in1=b), inc=(p.sV, 1))
        v_tok = p.sV.n
        p.emit("scalar", lambda e, o=r_bc[:, hs], a=r_bc[:, hs], b=ctx.sm32[:, 352:353]:
               e.activation(out=o, in_=a, func=AF.Ln, bias=b), waits=[(p.sV, v_tok)])
        p.emit("scalar", lambda e, o=r_bc[:, hs], a=r_bc[:, hs]:
               e.activation(out=o, in_=a, func=AF.Exp, scale=-0.5), inc=(p.sA, 1))
        r_tok = p.sA.n
        for dc in range(DC):
            if full:
                xo = ctx.xnf(dc)[:, th * TH + tc * 512: th * TH + (tc + 1) * 512]
            else:
                xo = ctx.xnh(th, dc)[:, tc * 512:(tc + 1) * 512]
            p.emit("vector", lambda e, o=scr, a=ctx.x(dc)[:, cs], b=m_bc[:, hs]:
                   e.tensor_sub(out=o, in0=a, in1=b),
                   waits=[(p.sA, r_tok)] if dc == 0 else [], inc=(p.sV, 1))
            p.emit("vector", lambda e, o=xo, a=scr, b=r_bc[:, hs]:
                   e.tensor_mul(out=o, in0=a, in1=b), inc=(p.sV, 1))
    ctx.guard[("gu_psum", 0)] = (p.sA, prev_evac)
    return p.sV.n


def emit_routing(ctx, th, xn_tok):
    p = ctx.p

    def pv(fn, waits=()):
        p.emit("vector", fn, waits=list(waits) + [(p.sV, p.sV.n)], inc=(p.sV, 1))
    gw = ctx.smbf[:, 256:320]
    logits = ctx.sm32[:, 88:216]
    WteT = ctx.sm32[:, 216:344]
    cols = ctx.sm32[:, 344:352]
    gw_tok = ctx.guard.get("gw_tok", 0)
    idf_tok = ctx.guard.get("idf_tok", 0)
    wte_read = ctx.guard.get("wte_read", 0)
    prev_exp = 0
    prev_cp = 0
    for g in range(8):
        pl = ctx.pb[2]
        for dc in range(DC):
            w0 = []
            if dc == 0:
                if g == 0:
                    w0 = [(p.sV, xn_tok), (p.sW, gw_tok)]
                if prev_exp:
                    w0.append((p.sA, prev_exp))
            p.emit("tensor", lambda e, o=pl[:, 0:8],
                   l=ctx.xnh(th, dc)[:, g * 128:(g + 1) * 128],
                   r=gw[:, dc * 8:dc * 8 + 8], st=(dc == 0), sp=(dc == DC - 1):
                   e.matmul(out=o, lhsT=l, rhs=r, start=st, stop=sp),
                   waits=w0, inc=(p.sT, 1) if dc == DC - 1 else None)
        mm_tok = p.sT.n
        gcol = slice(g * 8, g * 8 + 8)
        pv(lambda e, o=cols[:, 0:1], a=pl[:, 0:8]:
               e.reduce_max(out=o, in_=a, axis=AX, negate=True),
               waits=[(p.sT, mm_tok)])
        nm = p.sV.n
        p.emit("scalar", lambda e, o=logits[:, gcol], a=pl[:, 0:8], b=cols[:, 0:1], acc=cols[:, 1:2]:
               e.activation(out=o, in_=a, func=AF.Exp, bias=b, accum_out=acc),
               waits=[(p.sV, nm)], inc=(p.sA, 1))
        prev_exp = p.sA.n
        pv(lambda e, o=cols[:, 2:3], a=cols[:, 1:2]:
               e.reciprocal(out=o, in_=a), waits=[(p.sA, prev_exp)])
        p.emit("vector", lambda e, o=logits[:, gcol], a=logits[:, gcol], b=cols[:, 2:3]:
               e.tensor_scalar_mul(out=o, in0=a, scalar1=b))
        pv(lambda e, o=cols[:, 3:4], a=logits[:, gcol]:
               e.reduce_max(out=o, in_=a, axis=AX))
        pv(lambda e, o=WteT[:, gcol], a=logits[:, gcol], b=cols[:, 3:4]:
               e.tensor_scalar(out=o, in0=a, scalar1=b, scalar2=None, op0=ALU.is_ge))
        pv(lambda e, o=logits[:, gcol], a=WteT[:, gcol], b=logits[:, gcol]:
               e.scalar_tensor_tensor(out=o, in0=a, scalar=-2.0, in1=b,
                                      op0=ALU.mult, op1=ALU.add))
        pv(lambda e, o=cols[:, 4:5], a=logits[:, gcol]:
               e.reduce_max(out=o, in_=a, axis=AX))
        pv(lambda e, o=logits[:, gcol], a=logits[:, gcol], b=cols[:, 4:5]:
               e.tensor_scalar(out=o, in0=a, scalar1=b, scalar2=None, op0=ALU.is_ge))
        pv(lambda e, o=cols[:, 5:6], a=cols[:, 3:4], b=cols[:, 4:5]:
               e.tensor_add(out=o, in0=a, in1=b))
        pv(lambda e, o=cols[:, 5:6], a=cols[:, 5:6]:
               e.reciprocal(out=o, in_=a))
        pv(lambda e, o=cols[:, 6:7], a=cols[:, 3:4], b=cols[:, 5:6]:
               e.tensor_mul(out=o, in0=a, in1=b))
        pv(lambda e, o=cols[:, 6:7], a=cols[:, 6:7]:
               e.tensor_scalar_mul(out=o, in0=a, scalar1=0.5))
        pv(lambda e, o=cols[:, 7:8], a=cols[:, 4:5], b=cols[:, 5:6]:
               e.tensor_mul(out=o, in0=a, in1=b))
        pv(lambda e, o=cols[:, 7:8], a=cols[:, 7:8]:
               e.tensor_scalar_mul(out=o, in0=a, scalar1=0.5))
        pv(lambda e, o=WteT[:, gcol], a=WteT[:, gcol], b=cols[:, 6:7]:
               e.tensor_scalar_mul(out=o, in0=a, scalar1=b))
        pv(lambda e, o=WteT[:, gcol], a=logits[:, gcol], b=cols[:, 7:8], c=WteT[:, gcol]:
               e.scalar_tensor_tensor(out=o, in0=a, scalar=b, in1=c,
                                      op0=ALU.mult, op1=ALU.add))
        w_tok = p.sV.n
        tw = [(p.sV, w_tok)]
        if g == 0 and idf_tok:
            tw.append((p.sX, idf_tok))
        if prev_cp:
            tw.append((p.sA, prev_cp))
        p.emit("tensor", lambda e, o=ctx.pb[3][0:8, 0:128], i=WteT[:, gcol], idt=ctx.ident_f32[:]:
               e.transpose(out=o, in_=i, identity=idt), waits=tw, inc=(p.sT, 1))
        t_tok = p.sT.n
        cw = [(p.sT, t_tok)]
        if g == 0 and wte_read:
            cw.append((p.sT, wte_read))
        p.emit("scalar", lambda e, o=ctx.wte_em[0:8, g * 128:(g + 1) * 128],
               a=ctx.pb[3][0:8, 0:128]:
               e.copy(out=o, in_=a), waits=cw, inc=(p.sA, 1))
        prev_cp = p.sA.n
    ctx.guard[("gu_psum", 1)] = (p.sA, p.sA.n)
    return p.sA.n


FH = [(0, 768), (768, 1408)]
GU_OFF = [0, 12288]
WD_OFF = 22528


def emit_moe_half(ctx, wd_d, th, xn_tok, wte_tok, nexperts=10):
    p = ctx.p
    experts = ([("sh", i) for i in range(NSH)] + [("ex", i) for i in range(NEXP)])[:nexperts]
    sg_scr = [ctx.arena[:, 2048 + i * 512: 2048 + (i + 1) * 512] for i in range(2)]
    oh_tok = ctx.guard.get("oh_tok", 0)

    for ei, (kind, enum) in enumerate(experts):
        routed = kind == "ex"
        gname, uname, dname = (("sh_gate", "sh_up", "sh_down") if kind == "sh"
                               else ("exp_gate", "exp_up", "exp_down"))
        for fh in range(2):
            fs, fe = FH[fh]
            w = fe - fs
            off = GU_OFF[fh]
            waits = []
            pt = ctx.guard.get(("gu_done", fh), 0)
            if pt:
                waits = [(p.sT, pt)]
            eg = wd_d[gname][enum].rearrange("(dc p) f -> p dc f", p=128)[:, :, fs:fe]
            eu = wd_d[uname][enum].rearrange("(dc p) f -> p dc f", p=128)[:, :, fs:fe]
            p.dma("gpsimd", ctx.wbig[:, off:off + 8 * w]
                  .rearrange("p (dc f) -> p dc f", f=w), eg, waits=waits)
            t2 = p.dma("gpsimd", ctx.wbig[:, off + 8 * w:off + 16 * w]
                       .rearrange("p (dc f) -> p dc f", f=w), eu)
            ctx.guard[("gu_tok", fh)] = t2
        dwaits = []
        pt = ctx.guard.get("down_done", 0)
        if pt:
            dwaits = [(p.sT, pt)]
        ed = wd_d[dname][enum].rearrange("(fc p) d -> p fc d", p=128)
        t3 = p.dma("gpsimd", ctx.wbig[:, WD_OFF:WD_OFF + 11 * 1024]
                   .rearrange("p (fc d) -> p fc d", d=1024), ed, waits=dwaits)
        wd_tok = t3

        if routed:
            for tc in range(2):
                wv = [(p.sA, wte_tok)]
                if oh_tok:
                    wv.append((p.sW, oh_tok))
                prev = ctx.guard.get(("wte_free", tc), 0)
                if prev:
                    wv.append((p.sV, prev))
                p.emit("tensor", lambda e, o=ctx.pb[6 + tc][:],
                       l=ctx.onehot[:, enum * 128:(enum + 1) * 128],
                       r=ctx.wte_em[0:8, tc * 512:(tc + 1) * 512]:
                       e.matmul(out=o, lhsT=l, rhs=r, start=True, stop=True),
                       waits=wv, inc=(p.sT, 1))
                ctx.guard[("wte_bc", tc)] = p.sT.n

        down_prev = ctx.guard.get("down_done", 0)
        for fh in range(2):
            fs, fe = FH[fh]
            w = fe - fs
            nfc = w // 128
            off = GU_OFF[fh]
            gu_tok = ctx.guard[("gu_tok", fh)]
            for tc in range(2):
                for fcl in range(nfc):
                    fc = fs // 128 + fcl
                    slot = fcl % 2
                    pg = ctx.pb[slot * 2]
                    pu = ctx.pb[slot * 2 + 1]
                    mmw = [(p.sW, gu_tok), (p.sV, xn_tok)]
                    prev = ctx.guard.get(("gu_psum", slot))
                    if prev:
                        mmw.append(prev)
                    for dc in range(DC):
                        p.emit("tensor", lambda e, o=pg[:],
                               l=ctx.wbig[:, off + dc * w + fcl * 128: off + dc * w + fcl * 128 + 128],
                               r=ctx.xnh(th, dc)[:, tc * 512:(tc + 1) * 512],
                               st=(dc == 0), sp=(dc == DC - 1):
                               e.matmul(out=o, lhsT=l, rhs=r, start=st, stop=sp),
                               waits=mmw if dc == 0 else [])
                        p.emit("tensor", lambda e, o=pu[:],
                               l=ctx.wbig[:, off + 8 * w + dc * w + fcl * 128: off + 8 * w + dc * w + fcl * 128 + 128],
                               r=ctx.xnh(th, dc)[:, tc * 512:(tc + 1) * 512],
                               st=(dc == 0), sp=(dc == DC - 1):
                               e.matmul(out=o, lhsT=l, rhs=r, start=st, stop=sp),
                               inc=(p.sT, 1) if dc == DC - 1 else None)
                    gu_mm = p.sT.n
                    sw = [(p.sT, gu_mm)]
                    prevh = ctx.guard.get(("sg_free", slot), 0)
                    if prevh:
                        sw.append((p.sV, prevh))
                    p.emit("scalar", lambda e, o=sg_scr[slot], a=pg[:]:
                           e.activation(out=o, in_=a, func=AF.Silu),
                           waits=sw, inc=(p.sA, 1))
                    sg_tok = p.sA.n
                    hsl = ctx.hbig[:, fc * 1024 + tc * 512: fc * 1024 + (tc + 1) * 512]
                    hw = [(p.sA, sg_tok)]
                    if down_prev:
                        hw.append((p.sT, down_prev))
                    if routed:
                        hw.append((p.sT, ctx.guard[("wte_bc", tc)]))
                        p.emit("vector", lambda e, o=sg_scr[slot], a=sg_scr[slot], b=pu[:]:
                               e.tensor_mul(out=o, in0=a, in1=b), waits=hw, inc=(p.sV, 1))
                        p.emit("vector", lambda e, o=hsl, a=sg_scr[slot], b=ctx.pb[6 + tc][:]:
                               e.tensor_mul(out=o, in0=a, in1=b), inc=(p.sV, 1))
                    else:
                        p.emit("vector", lambda e, o=hsl, a=sg_scr[slot], b=pu[:]:
                               e.scalar_tensor_tensor(out=o, in0=a, scalar=0.5, in1=b,
                                                      op0=ALU.mult, op1=ALU.mult),
                               waits=hw, inc=(p.sV, 1))
                    ctx.guard[("gu_psum", slot)] = (p.sV, p.sV.n)
                    ctx.guard[("sg_free", slot)] = p.sV.n
            ctx.guard[("gu_done", fh)] = p.sT.n
        h_tok = p.sV.n
        if routed:
            for tc in range(2):
                ctx.guard[("wte_free", tc)] = h_tok

        for tc in range(2):
            cs = slice(th * TH + tc * 512, th * TH + (tc + 1) * 512)
            for dc in range(DC):
                pdw = ctx.pb[4 + dc % 2]
                dw = [(p.sV, h_tok), (p.sW, wd_tok)]
                prev = ctx.guard.get(("dw_psum", dc % 2), 0)
                if prev:
                    dw.append((p.sV, prev))
                for fc in range(11):
                    p.emit("tensor", lambda e, o=pdw[:],
                           l=ctx.wbig[:, WD_OFF + fc * 1024 + dc * 128: WD_OFF + fc * 1024 + dc * 128 + 128],
                           r=ctx.hbig[:, fc * 1024 + tc * 512: fc * 1024 + (tc + 1) * 512],
                           st=(fc == 0), sp=(fc == 10):
                           e.matmul(out=o, lhsT=l, rhs=r, start=st, stop=sp),
                           waits=dw if fc == 0 else [],
                           inc=(p.sT, 1) if fc == 10 else None)
                dmm = p.sT.n
                p.emit("vector", lambda e, o=ctx.x(dc)[:, cs], a=ctx.x(dc)[:, cs], b=pdw[:]:
                       e.tensor_add(out=o, in0=a, in1=b),
                       waits=[(p.sT, dmm)], inc=(p.sV, 1))
                ctx.guard[("dw_psum", dc % 2)] = p.sV.n
                ctx.x_tok[dc] = p.sV.n
        ctx.guard["down_done"] = p.sT.n
    ctx.guard["hbig_free"] = ctx.guard["down_done"]
    ctx.guard["wte_read"] = ctx.guard["down_done"]


def emit_moe_sublayer(ctx, wd_d, stage=99, nexperts=10, nth=2):
    for th in range(nth):
        xn_tok = emit_ln(ctx, th)
        if stage < 2:
            continue
        wte_tok = emit_routing(ctx, th, xn_tok)
        if stage < 3:
            continue
        emit_moe_half(ctx, wd_d, th, xn_tok, wte_tok, nexperts=nexperts)


def declare_moe_weights(nc):
    wd = {}
    wd["exp_gate"] = nc.dram_tensor("exp_gate", [NEXP, D, F_MOE], F32, kind="ExternalInput").ap()
    wd["exp_up"] = nc.dram_tensor("exp_up", [NEXP, D, F_MOE], F32, kind="ExternalInput").ap()
    wd["exp_down"] = nc.dram_tensor("exp_down", [NEXP, F_MOE, D], F32, kind="ExternalInput").ap()
    wd["sh_gate"] = nc.dram_tensor("sh_gate", [NSH, D, F_MOE], F32, kind="ExternalInput").ap()
    wd["sh_up"] = nc.dram_tensor("sh_up", [NSH, D, F_MOE], F32, kind="ExternalInput").ap()
    wd["sh_down"] = nc.dram_tensor("sh_down", [NSH, F_MOE, D], F32, kind="ExternalInput").ap()
    return wd


def emit_prologue(ctx, xin, need_moe=True):
    p = ctx.p
    nc = ctx.nc
    ident_d = nc.dram_tensor("ident", [128, 128], F32, kind="ExternalInput").ap()
    p.dma("sync", ctx.ident_f32[:], ident_d)
    for dc in range(DC):
        p.dma("sync", ctx.x(dc), xin[dc * 128:(dc + 1) * 128, :])
    ctx.guard["xload"] = p.sX.n
    ctx.guard["idf_tok"] = p.sX.n
    p.dma("gpsimd", ctx.smbf[:, 128:256], ident_d)
    if need_moe:
        gate_w_sb_d = nc.dram_tensor("gate_w_sb", [128, 64], F32, kind="ExternalInput").ap()
        onehot_d = nc.dram_tensor("onehot_bf", [8, 1024], F32, kind="ExternalInput").ap()
        p.dma("gpsimd", ctx.smbf[:, 256:320], gate_w_sb_d)
        p.dma("gpsimd", ctx.onehot[:], onehot_d)
    ctx.guard["gw_tok"] = p.sW.n
    ctx.guard["oh_tok"] = p.sW.n
    tid = p.sW.n
    p.emit("vector", lambda e: e.memset(ctx.smbf[:, 0:128], 1.0),
           waits=[(p.sX, ctx.guard["xload"]), (p.sW, tid)])
    p.emit("vector", lambda e: e.memset(ctx.sm32[:, 352:353], 1e-5), inc=(p.sV, 1))
    ctx.guard["consts_ready"] = p.sV.n


def emit_epilogue(ctx, xout):
    p = ctx.p
    toks = []
    for dc in range(DC):
        w = [(p.sV, ctx.x_tok[dc])]
        if dc == 0:
            w.append((p.sX, ctx.guard["xload"]))
        toks.append(p.dma("sync", xout[dc * 128:(dc + 1) * 128, :], ctx.x(dc), waits=w))
    p.wait("sync", p.sX, toks[-1])


def build_nc_moe(stage=99, nexperts=10, nth=2, debug=False):
    nc = bass.Bass("TRN2", target_bir_lowering=False, detect_race_conditions=False)
    p = Prog(nc)
    xin = nc.dram_tensor("xin", [D, T], F32, kind="ExternalInput").ap()
    xout = nc.dram_tensor("xout", [D, T], F32, kind="ExternalOutput").ap()
    wd = declare_moe_weights(nc)
    ctx = Ctx(nc, p)
    emit_prologue(ctx, xin)
    if stage >= 1:
        emit_moe_sublayer(ctx, wd, stage=stage, nexperts=nexperts, nth=nth)
    if debug:
        xnd = nc.dram_tensor("xn_dump", [128, DC * T], F32, kind="ExternalOutput").ap()
        wted = nc.dram_tensor("wte_dump", [8, 1024], F32, kind="ExternalOutput").ap()
        hd = nc.dram_tensor("h_dump", [128, 11264], F32, kind="ExternalOutput").ap()
        dbg_sb = ctx.arena[:, 0:2048]
        vtok0 = p.sV.n
        for blk in range(4 * nth):
            sl = slice(blk * 2048, (blk + 1) * 2048)
            p.emit("vector", lambda e, o=dbg_sb, a=ctx.xnbig[:, sl]:
                   e.tensor_copy(out=o, in_=a),
                   waits=[(p.sV, ctx.x_tok[0]), (p.sX, p.sX.n)], inc=(p.sV, 1))
            p.dma("sync", xnd[:, sl], dbg_sb, waits=[(p.sV, p.sV.n)])
        for blk in range(8):
            sl = slice(blk * 1408, (blk + 1) * 1408)
            p.emit("vector", lambda e, o=ctx.arena[:, 0:1408], a=ctx.hbig[:, sl]:
                   e.tensor_copy(out=o, in_=a),
                   waits=[(p.sX, p.sX.n)], inc=(p.sV, 1))
            p.dma("sync", hd[:, sl], ctx.arena[:, 0:1408], waits=[(p.sV, p.sV.n)])
        p.emit("vector", lambda e, o=ctx.arena[0:8, 0:1024], a=ctx.wte_em[:]:
               e.tensor_copy(out=o, in_=a), waits=[(p.sX, p.sX.n)], inc=(p.sV, 1))
        p.dma("sync", wted, ctx.arena[0:8, 0:1024], waits=[(p.sV, p.sV.n)])
    emit_epilogue(ctx, xout)
    p.build()
    return nc


def _host_constants():
    ident = np.eye(128, dtype=np.float32)
    onehot = np.zeros((8, 1024), np.float32)
    for e in range(8):
        onehot[e, e * 128:(e + 1) * 128] = 1.0
    return ident, onehot


def _rearrange_gate_w(gate_w):
    return np.ascontiguousarray(gate_w.reshape(8, 128, 8).transpose(1, 0, 2).reshape(128, 64))


def moe_in_maps(x_full, inputs):
    ident, onehot = _host_constants()
    gw = _rearrange_gate_w(np.asarray(inputs["L0_ffn_gate_w"], np.float32))
    maps = []
    for b in range(NCORE):
        maps.append({
            "xin": np.ascontiguousarray(np.asarray(x_full[b], np.float32).T),
            "gate_w_sb": gw,
            "onehot_bf": onehot,
            "ident": ident,
            "exp_gate": np.asarray(inputs["L0_ffn_exp_gate"], np.float32),
            "exp_up": np.asarray(inputs["L0_ffn_exp_up"], np.float32),
            "exp_down": np.asarray(inputs["L0_ffn_exp_down"], np.float32),
            "sh_gate": np.asarray(inputs["L0_ffn_sh_gate"], np.float32),
            "sh_up": np.asarray(inputs["L0_ffn_sh_up"], np.float32),
            "sh_down": np.asarray(inputs["L0_ffn_sh_down"], np.float32),
        })
    return maps


def run_moe_phase(x_full, inputs, stage=99, nexperts=10, nth=2):
    nc = build_nc_moe(stage=stage, nexperts=nexperts, nth=nth)
    res = run_bass_kernel_spmd(nc, moe_in_maps(x_full, inputs), core_ids=list(range(NCORE)))
    return np.stack([np.ascontiguousarray(r["xout"].T) for r in res.results])


def emit_attn0(ctx, wd):
    p = ctx.p
    QF, KF = 0, 16384
    WS = 5184
    xn_tok = ctx._attn_xn_tok
    hb_guard = ctx.guard.get("hbig_free", 0)
    p.dma("sync", ctx.sm32[:, 0:8], wd["bq_sb"])
    p.dma("sync", ctx.sm32[:, 8:16], wd["bk_sb"])
    btok = p.sX.n

    wname = {0: "Wq", 1: "Wk"}
    for pi, (ooff, boff) in enumerate(((QF, 0), (KF, 8))):
        for fc in range(8):
            i = pi * 8 + fc
            ws = ctx.hbig[:, WS + (i % 2) * 1024: WS + (i % 2) * 1024 + 1024]
            lw = []
            prev = ctx.guard.get(("a0w", i % 2))
            if prev:
                lw.append(prev)
            elif hb_guard:
                lw.append((p.sT, hb_guard))
            wtok = p.dma("gpsimd", ws.rearrange("p (dc f) -> p dc f", f=128),
                         wd[wname[pi]].rearrange("(dc p) f -> p dc f", p=128)[:, :, fc * 128:(fc + 1) * 128],
                         waits=lw)
            for qc in range(4):
                ps = ctx.pb[qc % 2]
                w0 = [(p.sW, wtok), (p.sV, xn_tok)]
                prev = ctx.guard.get(("a0p", qc % 2))
                if prev:
                    w0.append(prev)
                for dc in range(DC):
                    p.emit("tensor", lambda e, o=ps[:],
                           l=ws[:, dc * 128:(dc + 1) * 128],
                           r=ctx.xnf(dc)[:, qc * 512:(qc + 1) * 512],
                           st=(dc == 0), sp=(dc == DC - 1):
                           e.matmul(out=o, lhsT=l, rhs=r, start=st, stop=sp),
                           waits=w0 if dc == 0 else [],
                           inc=(p.sT, 1) if dc == DC - 1 else None)
                mm = p.sT.n
                if qc == 3:
                    ctx.guard[("a0w", i % 2)] = (p.sT, mm)
                p.emit("scalar", lambda e, o=ctx.wbig[:, ooff + fc * 2048 + qc * 512: ooff + fc * 2048 + (qc + 1) * 512],
                       a=ps[:], b=ctx.sm32[:, boff + fc:boff + fc + 1]:
                       e.add(out=o, in_=a, add=b),
                       waits=[(p.sT, mm), (p.sX, btok)], inc=(p.sA, 1))
                ctx.guard[("a0p", qc % 2)] = (p.sA, p.sA.n)
    qk_tok = p.sA.n

    def q_head(h, sl):
        return ctx.wbig[(h % 2) * 64:(h % 2) * 64 + 64, QF + (h // 2) * 2048 + sl.start: QF + (h // 2) * 2048 + sl.stop]

    def k_head(h, sl):
        return ctx.wbig[(h % 2) * 64:(h % 2) * 64 + 64, KF + (h // 2) * 2048 + sl.start: KF + (h // 2) * 2048 + sl.stop]

    hw = [(p.sT, hb_guard)] if hb_guard else []
    for kb in range(2):
        p.emit("vector", lambda e, kb=kb:
               e.memset(ctx.hbig[:, kb * 1040:(kb + 1) * 1040]
                        .rearrange("p (kc c) -> p kc c", c=65)[:, :, 64:65], 1.0),
               waits=hw, inc=(p.sV, 1))
    ones_tok = p.sV.n

    for h in range(NH):
        kb = h % 2
        kaug = ctx.hbig[:, kb * 1040:(kb + 1) * 1040]
        prev_use = ctx.guard.get(("kaug", kb))
        pp0 = (h % 2) * 64
        idt_sl = ctx.smbf[pp0:pp0 + 64, 128 + pp0:128 + pp0 + 64]
        for kc in range(16):
            tw = [(p.sA, qk_tok), (p.sV, ones_tok)]
            prev_t = ctx.guard.get("a0_tr")
            if prev_t:
                tw.append(prev_t)
            p.emit("tensor", lambda e, o=ctx.pb[3][:].bitcast(BF16)[:, 0:64],
                   i=k_head(h, slice(kc * 128, kc * 128 + 128)), idt=idt_sl:
                   e.transpose(out=o, in_=i, identity=idt),
                   waits=tw, inc=(p.sT, 1))
            ttok = p.sT.n
            cw = [(p.sT, ttok)]
            if prev_use and kc == 0:
                cw.append(prev_use)
            p.emit("scalar", lambda e, o=kaug[:, kc * 65:kc * 65 + 64], a=ctx.pb[3][:].bitcast(BF16)[:, 0:64]:
                   e.copy(out=o, in_=a), waits=cw, inc=(p.sA, 1))
            ctx.guard["a0_tr"] = (p.sA, p.sA.n)
        kaug_tok = p.sA.n

        for qc in range(4):
            qsl = slice(qc * 512, (qc + 1) * 512)
            po = ctx.pb[4 + qc % 2]
            prev_o = ctx.guard.get(("a0out", qc % 2))
            for kc in range(16):
                pst = ctx.pb[kc % 2]
                es = ctx.hbig[:, 2080 + (kc % 2) * 512: 2080 + (kc % 2) * 512 + 512]
                stw = [(p.sA, kaug_tok)]
                prev_e = ctx.guard.get(("a0p2", kc % 2))
                if prev_e:
                    stw.append(prev_e)
                p.emit("tensor", lambda e, o=pst[:], l=k_head(h, slice(kc * 128, kc * 128 + 128)),
                       r=q_head(h, qsl):
                       e.matmul(out=o, lhsT=l, rhs=r, start=True, stop=True),
                       waits=stw, inc=(p.sT, 1))
                st_tok = p.sT.n
                ew0 = [(p.sT, st_tok)]
                prev_es = ctx.guard.get(("a0e", kc % 2))
                if prev_es:
                    ew0.append(prev_es)
                p.emit("scalar", lambda e, o=es, a=pst[:]:
                       e.activation(out=o, in_=a, func=AF.Exp, scale=1.0 / 32.0),
                       waits=ew0, inc=(p.sA, 1))
                e_tok = p.sA.n
                ctx.guard[("a0p2", kc % 2)] = (p.sA, e_tok)
                vw = [(p.sA, e_tok)]
                if kc == 0 and prev_o:
                    vw.append(prev_o)
                p.emit("tensor", lambda e, o=po[0:65, :], l=kaug[:, kc * 65:(kc + 1) * 65],
                       r=es, st=(kc == 0), sp=(kc == 15):
                       e.matmul(out=o, lhsT=l, rhs=r, start=st, stop=sp),
                       waits=vw, inc=(p.sT, 1))
                ctx.guard[("a0e", kc % 2)] = (p.sT, p.sT.n)
            o_tok = p.sT.n
            p.emit("vector", lambda e, o=ctx.arena[64:65, 1024:1536], a=po[64:65, :]:
                   e.reciprocal(out=o, in_=a), waits=[(p.sT, o_tok)], inc=(p.sV, 1))
            p.emit("vector", lambda e, o=ctx.smbf[64:65, 512:1024], a=ctx.arena[64:65, 1024:1536]:
                   e.tensor_copy(out=o, in_=a), waits=[(p.sV, p.sV.n)], inc=(p.sV, 1))
            r_tok = p.sV.n
            prev_rb = ctx.guard.get("a0rb")
            rw = [(p.sV, r_tok)]
            if prev_rb:
                rw.append(prev_rb)
            p.emit("tensor", lambda e, o=ctx.pb[6][:], l=ctx.smbf[64:65, 0:128],
                   r=ctx.smbf[64:65, 512:1024]:
                   e.matmul(out=o, lhsT=l, rhs=r, start=True, stop=True),
                   waits=rw, inc=(p.sT, 1))
            rb_tok = p.sT.n
            p.emit("scalar", lambda e, o=ctx.arena[0:64, 0:512], a=po[0:64, :]:
                   e.copy(out=o, in_=a), waits=[(p.sT, rb_tok)], inc=(p.sA, 1))
            ev_tok = p.sA.n
            dc = h // 2
            pp = (h % 2) * 64
            if pp == 0:
                p.emit("vector", lambda e, o=ctx.arena[0:64, 512:1024], a=ctx.arena[0:64, 0:512], b=ctx.pb[6][0:64, :]:
                       e.tensor_mul(out=o, in0=a, in1=b),
                       waits=[(p.sA, ev_tok)], inc=(p.sV, 1))
                y_tok = p.sV.n
                ctx.guard["a0rb"] = (p.sV, y_tok)
                p.emit("vector", lambda e, o=ctx.x(dc)[0:64, qsl], a=ctx.x(dc)[0:64, qsl],
                       b=ctx.arena[0:64, 512:1024]:
                       e.tensor_add(out=o, in0=a, in1=b),
                       waits=[(p.sV, y_tok)], inc=(p.sV, 1))
            else:
                p.emit("vector", lambda e, o=ctx.smbf[0:64, 512:1024], a=ctx.arena[0:64, 0:512], b=ctx.pb[6][0:64, :]:
                       e.tensor_mul(out=o, in0=a, in1=b),
                       waits=[(p.sA, ev_tok)], inc=(p.sV, 1))
                y_tok = p.sV.n
                ctx.guard["a0rb"] = (p.sV, y_tok)
                prev_rl = ctx.guard.get("a0rl")
                rl = [(p.sV, y_tok)]
                if prev_rl:
                    rl.append(prev_rl)
                p.emit("tensor", lambda e, o=ctx.pb[7][64:128, :], l=ctx.smbf[0:64, 128:192],
                       r=ctx.smbf[0:64, 512:1024]:
                       e.matmul(out=o, lhsT=l, rhs=r, start=True, stop=True,
                                tile_position=(0, 64)),
                       waits=rl, inc=(p.sT, 1))
                rl_tok = p.sT.n
                p.emit("vector", lambda e, o=ctx.x(dc)[64:128, qsl], a=ctx.x(dc)[64:128, qsl],
                       b=ctx.pb[7][64:128, :]:
                       e.tensor_add(out=o, in0=a, in1=b),
                       waits=[(p.sT, rl_tok)], inc=(p.sV, 1))
                ctx.guard["a0rl"] = (p.sV, p.sV.n)
            ctx.x_tok[dc] = p.sV.n
            ctx.guard[("a0out", qc % 2)] = (p.sA, ev_tok)
        ctx.guard[("kaug", kb)] = (p.sT, p.sT.n)
    ctx.guard["hbig_free"] = p.sT.n
    ctx.guard["gu_done"] = 0
    ctx.guard["down_done"] = p.sT.n


def emit_attn0_sublayer(ctx, wd):
    t0 = emit_ln(ctx, 0, full=True)
    t1 = emit_ln(ctx, 1, full=True)
    ctx._attn_xn_tok = t1
    emit_attn0(ctx, wd)


def declare_attn0_weights(nc):
    wd = {}
    wd["Wq"] = nc.dram_tensor("Wq", [D, D], F32, kind="ExternalInput").ap()
    wd["Wk"] = nc.dram_tensor("Wk", [D, D], F32, kind="ExternalInput").ap()
    wd["bq_sb"] = nc.dram_tensor("bq_sb", [128, 8], F32, kind="ExternalInput").ap()
    wd["bk_sb"] = nc.dram_tensor("bk_sb", [128, 8], F32, kind="ExternalInput").ap()
    return wd


def build_nc_attn0():
    nc = bass.Bass("TRN2", target_bir_lowering=False, detect_race_conditions=False)
    p = Prog(nc)
    xin = nc.dram_tensor("xin", [D, T], F32, kind="ExternalInput").ap()
    xout = nc.dram_tensor("xout", [D, T], F32, kind="ExternalOutput").ap()
    wd = declare_attn0_weights(nc)
    ctx = Ctx(nc, p)
    emit_prologue(ctx, xin, need_moe=False)
    emit_attn0_sublayer(ctx, wd)
    emit_epilogue(ctx, xout)
    p.build()
    return nc


def attn0_in_maps(x_full, inputs):
    ident, _ = _host_constants()
    bq = np.asarray(inputs["L0_attn_bq"], np.float32).reshape(8, 128).T.copy()
    bk = np.asarray(inputs["L0_attn_bk"], np.float32).reshape(8, 128).T.copy()
    maps = []
    for b in range(NCORE):
        maps.append({
            "xin": np.ascontiguousarray(np.asarray(x_full[b], np.float32).T),
            "ident": ident,
            "Wq": np.asarray(inputs["L0_attn_Wq"], np.float32),
            "Wk": np.asarray(inputs["L0_attn_Wk"], np.float32),
            "bq_sb": np.ascontiguousarray(bq),
            "bk_sb": np.ascontiguousarray(bk),
        })
    return maps


def run_attn0_phase(x_full, inputs):
    nc = build_nc_attn0()
    res = run_bass_kernel_spmd(nc, attn0_in_maps(x_full, inputs), core_ids=list(range(NCORE)))
    return np.stack([np.ascontiguousarray(r["xout"].T) for r in res.results])


MAGIC = float(2 ** 22)


def emit_quant(ctx, dst, src, W, scale_const):
    p = ctx.p
    nb = W // 16
    y = ctx.arena[:, 0:W]
    a = ctx.arena[:, 512:512 + W]
    c = ctx.arena[:, 1024:1024 + W]
    q = ctx.arena[:, 1536:1536 + W]
    bm = ctx.sm32[:, 64:64 + nb]
    rs = ctx.sm32[:, 96:96 + nb]

    def pv(fn, waits=()):
        p.emit("vector", fn, waits=list(waits) + [(p.sV, p.sV.n)], inc=(p.sV, 1))

    d3 = lambda ap: ap.rearrange("p (nb b) -> p nb b", b=16)
    pv(lambda e, o=bm.unsqueeze(2), i=d3(src):
       e.reduce_max(out=o, in_=i, axis=AX, apply_absolute_value=True))
    pv(lambda e, o=rs, i=bm: e.reciprocal(out=o, in_=i))
    pv(lambda e, o=d3(y), i=d3(src), b=rs.unsqueeze(2).broadcast_to([128, nb, 16]):
       e.scalar_tensor_tensor(out=o, in0=i, scalar=6.0, in1=b, op0=ALU.mult, op1=ALU.mult))
    y_tok = p.sV.n
    aw = [(p.sV, y_tok)]
    prev = ctx.guard.get("qact")
    if prev:
        aw.append(prev)
    p.emit("scalar", lambda e, o=a, i=y: e.activation(out=o, in_=i, func=AF.Abs),
           waits=aw, inc=(p.sA, 1))
    abs_tok = p.sA.n
    pv(lambda e, o=q, i=a: e.tensor_scalar_add(out=o, in0=i, scalar1=MAGIC),
       waits=[(p.sA, abs_tok)])
    pv(lambda e, o=q, i=q: e.tensor_scalar(out=o, in0=i, scalar1=MAGIC, scalar2=2.0,
                                           op0=ALU.subtract, op1=ALU.min))
    pv(lambda e, o=c, i=a: e.tensor_single_scalar(out=o, in_=i, scalar=2.5, op=ALU.is_gt))
    pv(lambda e, o=c, i=a, b=c: e.scalar_tensor_tensor(out=o, in0=i, scalar=3.5, in1=b,
                                                       op0=ALU.is_gt, op1=ALU.add))
    pv(lambda e, o=c, i=a, b=c: e.scalar_tensor_tensor(out=o, in0=i, scalar=5.0, in1=b,
                                                       op0=ALU.is_gt, op1=ALU.add))
    pv(lambda e, o=c, i=a, b=c: e.scalar_tensor_tensor(out=o, in0=i, scalar=5.0, in1=b,
                                                       op0=ALU.is_gt, op1=ALU.add))
    pv(lambda e, o=q, i=q, b=c: e.tensor_add(out=o, in0=i, in1=b))
    q_tok = p.sV.n
    p.emit("scalar", lambda e, o=c, i=y: e.sign(out=o, in_=i),
           waits=[(p.sV, q_tok)], inc=(p.sA, 1))
    sgn_tok = p.sA.n
    ctx.guard["qact"] = (p.sV, 0)
    pv(lambda e, o=q, i=q, b=c: e.tensor_mul(out=o, in0=i, in1=b),
       waits=[(p.sA, sgn_tok)])
    pv(lambda e, o=d3(dst), i=d3(q), b=bm.unsqueeze(2).broadcast_to([128, nb, 16]),
       s=scale_const:
       e.scalar_tensor_tensor(out=o, in0=i, scalar=s, in1=b, op0=ALU.mult, op1=ALU.mult))
    ctx.guard["qact"] = (p.sV, p.sV.n)
    return p.sV.n


def emit_wprep(ctx, wd):
    p = ctx.p
    wf = ctx.wbig[:].bitcast(F32)
    jobs = [("w_gate", "wgq_nat", 22, 1024, 1.0 / 6.0),
            ("w_up", "wuq_nat", 22, 1024, 1.0 / 6.0),
            ("w_down", "wdq_nat", 8, 2816, 1.0 / 12.0)]
    i = 0
    for src_name, dst_name, nch, W, sc in jobs:
        for ch in range(nch):
            sl = i % 2
            fsrc = wf[:, sl * 2816: sl * 2816 + W]
            bdst = ctx.wbig[:, 11264 + sl * 3072: 11264 + sl * 3072 + W]
            lw = []
            prev = ctx.guard.get(("wp", sl))
            if prev:
                lw.append(prev)
            lt = p.dma("sync", fsrc, wd[src_name][ch * 128:(ch + 1) * 128, :], waits=lw)
            p.wait("vector", p.sX, lt)
            for off in range(0, W, 512):
                w = min(512, W - off)
                emit_quant(ctx, bdst[:, off:off + w], fsrc[:, off:off + w], w, sc)
            qt = p.sV.n
            ctx.guard[("wp", sl)] = (p.sV, qt)
            st = p.dma("sync", wd[dst_name][ch * 128:(ch + 1) * 128, :], bdst,
                       waits=[(p.sV, qt)])
            ctx.guard[("wp_st", sl)] = st
            i += 1
    ctx.guard["wprep_done"] = p.sX.n


GGR = [(0, 512), (512, 1024), (1024, 1536), (1536, 2048), (2048, 2560), (2560, 2816)]


def emit_ffn1(ctx, wd, tag, stage=99, banks=(0, 2, 6)):
    p = ctx.p
    if stage == 10:
        return
    wprep = ctx.guard.get("wprep_done", 0)
    hb = ctx.hbig[:].bitcast(F32)
    t32b, tbfb, pdwb = banks

    hb_guard = ctx.guard.get("hbig_free", 0)
    xn_guard = ctx.guard.get("xn_free", 0)
    pb_base = t32b
    for tg in range(16):
        tsl = slice(tg * 128, (tg + 1) * 128)
        x_tm = ctx.arena[:, 2048:3072]
        for dc in range(DC):
            tw = [(p.sV, ctx.x_tok[dc])] if tg == 0 else []
            prev = ctx.guard.get(("f1tr", dc % 2))
            if prev:
                tw.append(prev)
            prev_bq = ctx.guard.get(("f1tq", dc % 2))
            if prev_bq:
                tw.append(prev_bq)
            if tg == 0 and dc == 0:
                if hb_guard:
                    tw.append((p.sT, hb_guard))
                tw.append((p.sX, ctx.guard.get("xload", 0)))
                tw.append((p.sV, ctx.guard.get("consts_ready", 0)))
            p.emit("tensor", lambda e, o=ctx.pb[pb_base + dc % 2][:, 0:128], i=ctx.x(dc)[:, tsl],
                   idt=ctx.ident_f32[:]:
                   e.transpose(out=o, in_=i, identity=idt), waits=tw, inc=(p.sT, 1))
            ttok = p.sT.n
            if stage == 21:
                ctx.guard[("f1tr", dc % 2)] = (p.sT, ttok)
                continue
            cw = [(p.sT, ttok)]
            prev_q = ctx.guard.get("f1xtm")
            if prev_q and dc == 0:
                cw.append(prev_q)
            p.emit("scalar", lambda e, o=x_tm[:, dc * 128:(dc + 1) * 128], a=ctx.pb[pb_base + dc % 2][:, 0:128]:
                   e.copy(out=o, in_=a), waits=cw, inc=(p.sA, 1))
            ctx.guard[("f1tr", dc % 2)] = (p.sA, p.sA.n)
        if stage == 21:
            continue
        xtm_tok = p.sA.n
        if stage in (11, 22):
            ctx.guard["f1xtm"] = (p.sA, xtm_tok)
            continue
        s_col = ctx.sm32[:, 128:129]
        v_col = ctx.sm32[:, 129:130]
        r_col = ctx.sm32[:, 130:131]
        dump = hb[:, 4096:5120]
        p.emit("vector", lambda e, o=s_col, i=x_tm:
               e.reduce_sum(out=o, in_=i, axis=AX),
               waits=[(p.sA, xtm_tok), (p.sV, p.sV.n)], inc=(p.sV, 1))
        sv = p.sV.n
        dw = [(p.sA, xtm_tok)]
        prev_d = ctx.guard.get("f1dump")
        if prev_d:
            dw.append(prev_d)
        p.emit("scalar", lambda e, o=dump, i=x_tm, acc=v_col:
               e.activation(out=o, in_=i, func=AF.Square, accum_out=acc),
               waits=dw, inc=(p.sA, 1))
        sq_tok = p.sA.n
        ctx.guard["f1dump"] = (p.sA, sq_tok)
        p.emit("vector", lambda e, o=s_col, i=s_col:
               e.tensor_scalar_mul(out=o, in0=i, scalar1=1.0 / D),
               waits=[(p.sV, sv)], inc=(p.sV, 1))
        p.emit("vector", lambda e, o=v_col, i=v_col:
               e.tensor_scalar_mul(out=o, in0=i, scalar1=1.0 / D),
               waits=[(p.sA, sq_tok), (p.sV, p.sV.n)], inc=(p.sV, 1))
        p.emit("vector", lambda e, o=r_col, i=s_col, b=s_col:
               e.tensor_mul(out=o, in0=i, in1=b), waits=[(p.sV, p.sV.n)], inc=(p.sV, 1))
        p.emit("vector", lambda e, o=v_col, i=v_col, b=r_col:
               e.tensor_sub(out=o, in0=i, in1=b), waits=[(p.sV, p.sV.n)], inc=(p.sV, 1))
        vv = p.sV.n
        p.emit("scalar", lambda e, o=r_col, i=v_col, b=ctx.sm32[:, 352:353]:
               e.activation(out=o, in_=i, func=AF.Ln, bias=b),
               waits=[(p.sV, vv)], inc=(p.sA, 1))
        p.emit("scalar", lambda e, o=r_col, i=r_col:
               e.activation(out=o, in_=i, func=AF.Exp, scale=-0.5), inc=(p.sA, 1))
        rtok = p.sA.n
        p.emit("vector", lambda e, o=x_tm, i=x_tm, m=s_col, r=r_col:
               e.tensor_scalar(out=o, in0=i, scalar1=m, scalar2=r,
                               op0=ALU.subtract, op1=ALU.mult),
               waits=[(p.sA, rtok), (p.sV, p.sV.n)], inc=(p.sV, 1))
        if stage == 12:
            ctx.guard["f1xtm"] = (p.sV, p.sV.n)
            continue
        xq_tm = ctx.hbig[:, 10240:11264]
        for off in (0, 512):
            emit_quant(ctx, xq_tm[:, off:off + 512], x_tm[:, off:off + 512], 512, 1.0 / 6.0)
        q_tok = p.sV.n
        ctx.guard["f1xtm"] = (p.sV, q_tok)
        if stage == 13:
            continue
        for dc in range(DC):
            tw = [(p.sV, q_tok)]
            prev = ctx.guard.get(("f1tq", dc % 2))
            if prev:
                tw.append(prev)
            if tg == 0 and dc == 0 and xn_guard:
                tw.append((p.sT, xn_guard))
            p.emit("tensor", lambda e, o=ctx.pb[tbfb + dc % 2][:].bitcast(BF16)[:, 0:128],
                   i=xq_tm[:, dc * 128:(dc + 1) * 128], idt=ctx.smbf[:, 128:256]:
                   e.transpose(out=o, in_=i, identity=idt), waits=tw, inc=(p.sT, 1))
            ttok = p.sT.n
            p.emit("scalar", lambda e, o=ctx.xnf(dc)[:, tsl], a=ctx.pb[tbfb + dc % 2][:].bitcast(BF16)[:, 0:128]:
                   e.copy(out=o, in_=a), waits=[(p.sT, ttok)], inc=(p.sA, 1))
            ctx.guard[("f1tq", dc % 2)] = (p.sA, p.sA.n)
        ctx.guard[("f1xq_tg", tg)] = p.sA.n
    xq_tok = p.sA.n
    if stage < 2 or stage in (11, 12, 13, 21, 22):
        return

    dw = []
    prev = ctx.guard.get("wdq_free")
    if prev:
        dw.append(prev)
    if wprep:
        dw.append((p.sX, wprep))
    for fc in range(22):
        p.dma("sync", ctx.wbig[:, fc * 1024:(fc + 1) * 1024],
              wd["wdq_nat"][:, fc * 128:(fc + 1) * 128], waits=dw if fc == 0 else (),
              transpose=True)
    wdq_tok = p.sX.n

    for gi, (fs, fe) in enumerate(GGR):
        w = fe - fs
        sl = gi % 2
        wg_sb = ctx.wbig[:, 22528 + sl * 4096: 22528 + sl * 4096 + 8 * 512]
        wu_sb = ctx.hbig[:, sl * 4096: sl * 4096 + 8 * 512]
        lw = []
        prev = ctx.guard.get(("f1g", sl))
        if prev:
            lw.append(prev)
        if gi == 0 and wprep:
            lw.append((p.sX, wprep))
        for hc in range(DC):
            p.dma("sync", wg_sb[:, hc * 512:hc * 512 + w],
                  wd["wgq_nat"][fs:fe, hc * 128:(hc + 1) * 128],
                  waits=lw if hc == 0 else (), transpose=True)
            p.dma("sync", wu_sb[:, hc * 512:hc * 512 + w],
                  wd["wuq_nat"][fs:fe, hc * 128:(hc + 1) * 128], transpose=True)
        gran_tok = p.sX.n
        for tg in range(16):
            tsl = slice(tg * 128, (tg + 1) * 128)
            pg = ctx.pb[4]
            pu = ctx.pb[5]
            xq_wait = ctx.guard[("f1xq_tg", tg)] if (t32b != 4 and tbfb != 4) else xq_tok
            mw = [(p.sX, gran_tok), (p.sA, xq_wait)]
            prev = ctx.guard.get("f1gu")
            if prev:
                mw.append(prev)
            for dc in range(DC):
                p.emit("tensor", lambda e, o=pg[:, 0:w], l=ctx.xnf(dc)[:, tsl],
                       r=wg_sb[:, dc * 512:dc * 512 + w], st=(dc == 0), sp=(dc == DC - 1):
                       e.matmul(out=o, lhsT=l, rhs=r, start=st, stop=sp),
                       waits=mw if dc == 0 else [])
                p.emit("tensor", lambda e, o=pu[:, 0:w], l=ctx.xnf(dc)[:, tsl],
                       r=wu_sb[:, dc * 512:dc * 512 + w], st=(dc == 0), sp=(dc == DC - 1):
                       e.matmul(out=o, lhsT=l, rhs=r, start=st, stop=sp),
                       inc=(p.sT, 1) if dc == DC - 1 else None)
            gu_mm = p.sT.n
            if tg == 15:
                ctx.guard[("f1g", sl)] = (p.sT, gu_mm)
            sg = hb[:, 4096:4608]
            htm = hb[:, 4608:5120]
            sw = [(p.sT, gu_mm)]
            prev = ctx.guard.get("f1sg")
            if prev:
                sw.append(prev)
            p.emit("scalar", lambda e, o=sg[:, 0:w], a=pg[:, 0:w]:
                   e.activation(out=o, in_=a, func=AF.Silu), waits=sw, inc=(p.sA, 1))
            sg_tok = p.sA.n
            p.emit("vector", lambda e, o=htm[:, 0:w], a=sg[:, 0:w], b=pu[:, 0:w]:
                   e.tensor_mul(out=o, in0=a, in1=b),
                   waits=[(p.sA, sg_tok), (p.sV, p.sV.n)], inc=(p.sV, 1))
            ctx.guard["f1gu"] = (p.sV, p.sV.n)
            ctx.guard["f1sg"] = (p.sV, p.sV.n)
            hq_st = ctx.wbig[:, 30720 + (tg % 2) * 512: 30720 + (tg % 2) * 512 + 512]
            prev_st = ctx.guard.get(("f1hst", tg % 2), 0)
            if prev_st:
                p.wait("vector", p.sX, prev_st)
            emit_quant(ctx, hq_st[:, 0:w], htm[:, 0:w], w, 1.0 / 6.0)
            stok = p.dma("sync", wd["hq_dram"][tsl, fs:fe], hq_st[:, 0:w],
                         waits=[(p.sV, p.sV.n)])
            ctx.guard[("f1hst", tg % 2)] = stok
    hq_all = p.sX.n
    if stage < 3:
        return

    for tc in range(4):
        csl = slice(tc * 512, (tc + 1) * 512)
        lw = [(p.sX, hq_all)]
        prev = ctx.guard.get("f1hqT")
        if prev:
            lw.append(prev)
        for fc in range(22):
            p.dma("sync", ctx.hbig[:, fc * 512:(fc + 1) * 512],
                  wd["hq_dram"][csl, fc * 128:(fc + 1) * 128],
                  waits=lw if fc == 0 else (), transpose=True)
        hqT_tok = p.sX.n
        for dc in range(DC):
            pdw = ctx.pb[pdwb + dc % 2]
            dwx = [(p.sX, hqT_tok), (p.sX, wdq_tok)]
            prev = ctx.guard.get(("f1dw", dc % 2))
            if prev:
                dwx.append(prev)
            for fc in range(22):
                p.emit("tensor", lambda e, o=pdw[:],
                       l=ctx.wbig[:, fc * 1024 + dc * 128: fc * 1024 + dc * 128 + 128],
                       r=ctx.hbig[:, fc * 512:(fc + 1) * 512],
                       st=(fc == 0), sp=(fc == 21):
                       e.matmul(out=o, lhsT=l, rhs=r, start=st, stop=sp),
                       waits=dwx if fc == 0 else [],
                       inc=(p.sT, 1) if fc == 21 else None)
            dmm = p.sT.n
            if dc == DC - 1:
                ctx.guard["f1hqT"] = (p.sT, dmm)
            p.emit("vector", lambda e, o=ctx.x(dc)[:, csl], a=ctx.x(dc)[:, csl], b=pdw[:]:
                   e.tensor_add(out=o, in0=a, in1=b),
                   waits=[(p.sT, dmm), (p.sV, p.sV.n)], inc=(p.sV, 1))
            ctx.guard[("f1dw", dc % 2)] = (p.sV, p.sV.n)
            ctx.x_tok[dc] = p.sV.n
    ctx.guard["wdq_free"] = ctx.guard["f1hqT"]
    ctx.guard["hbig_free"] = p.sT.n
    ctx.guard["xn_free"] = p.sT.n
    ctx.guard["gu_done"] = 0
    ctx.guard["down_done"] = p.sT.n


def declare_ffn1_weights(nc, with_prep):
    wd = {}
    if with_prep:
        wd["w_gate"] = nc.dram_tensor("w_gate", [F_L1, D], F32, kind="ExternalInput").ap()
        wd["w_up"] = nc.dram_tensor("w_up", [F_L1, D], F32, kind="ExternalInput").ap()
        wd["w_down"] = nc.dram_tensor("w_down", [D, F_L1], F32, kind="ExternalInput").ap()
    kind = "Internal" if with_prep else "ExternalInput"
    wd["wgq_nat"] = nc.dram_tensor("wgq_nat", [F_L1, D], BF16, kind=kind).ap()
    wd["wuq_nat"] = nc.dram_tensor("wuq_nat", [F_L1, D], BF16, kind=kind).ap()
    wd["wdq_nat"] = nc.dram_tensor("wdq_nat", [D, F_L1], BF16, kind=kind).ap()
    wd["hq_dram"] = nc.dram_tensor("hq_dram", [T, F_L1], BF16, kind="Internal").ap()
    return wd


def build_nc_ffn1():
    nc = bass.Bass("TRN2", target_bir_lowering=False, detect_race_conditions=False)
    p = Prog(nc)
    xin = nc.dram_tensor("xin", [D, T], F32, kind="ExternalInput").ap()
    xout = nc.dram_tensor("xout", [D, T], F32, kind="ExternalOutput").ap()
    wd = declare_ffn1_weights(nc, with_prep=True)
    ctx = Ctx(nc, p)
    emit_prologue(ctx, xin, need_moe=False)
    emit_wprep(ctx, wd)
    emit_ffn1(ctx, wd, "f1")
    emit_epilogue(ctx, xout)
    p.build()
    return nc


def ffn1_in_maps(x_full, inputs):
    ident, _ = _host_constants()
    maps = []
    for b in range(NCORE):
        maps.append({
            "xin": np.ascontiguousarray(np.asarray(x_full[b], np.float32).T),
            "ident": ident,
            "w_gate": np.asarray(inputs["L1_ffn_w_gate"], np.float32),
            "w_up": np.asarray(inputs["L1_ffn_w_up"], np.float32),
            "w_down": np.asarray(inputs["L1_ffn_w_down"], np.float32),
        })
    return maps


def run_ffn1_phase(x_full, inputs):
    nc = build_nc_ffn1()
    res = run_bass_kernel_spmd(nc, ffn1_in_maps(x_full, inputs), core_ids=list(range(NCORE)))
    return np.stack([np.ascontiguousarray(r["xout"].T) for r in res.results])


def emit_attn1(ctx, wd):
    p = ctx.p
    QF, KF = 0, 16384
    xn_tok = ctx._attn_xn_tok
    hb_guard = ctx.guard.get("hbig_free", 0)
    hbf = ctx.hbig[:].bitcast(F32)
    abf = ctx.arena[:].bitcast(BF16)

    wname = "w_qkvT"
    for fc in range(24):
        ws = ctx.hbig[:, 8192 + (fc % 2) * 1024: 8192 + (fc % 2) * 1024 + 1024]
        lw = []
        prev = ctx.guard.get(("a1w", fc % 2))
        if prev:
            lw.append(prev)
        elif hb_guard:
            lw.append((p.sT, hb_guard))
        wtok = p.dma("gpsimd", ws.rearrange("p (dc f) -> p dc f", f=128),
                     wd[wname].rearrange("(dc p) f -> p dc f", p=128)[:, :, fc * 128:(fc + 1) * 128],
                     waits=lw)
        for qc in range(4):
            ps = ctx.pb[qc % 2]
            w0 = [(p.sW, wtok), (p.sV, xn_tok)]
            prev = ctx.guard.get(("a1p", qc % 2))
            if prev:
                w0.append(prev)
            for dc in range(DC):
                p.emit("tensor", lambda e, o=ps[:], l=ws[:, dc * 128:(dc + 1) * 128],
                       r=ctx.xnf(dc)[:, qc * 512:(qc + 1) * 512],
                       st=(dc == 0), sp=(dc == DC - 1):
                       e.matmul(out=o, lhsT=l, rhs=r, start=st, stop=sp),
                       waits=w0 if dc == 0 else [],
                       inc=(p.sT, 1) if dc == DC - 1 else None)
            mm = p.sT.n
            if qc == 3:
                ctx.guard[("a1w", fc % 2)] = (p.sT, mm)
            if fc < 16:
                ooff = QF if fc < 8 else KF
                fl = fc % 8
                p.emit("scalar", lambda e, o=ctx.wbig[:, ooff + fl * 2048 + qc * 512: ooff + fl * 2048 + (qc + 1) * 512],
                       a=ps[:]:
                       e.copy(out=o, in_=a), waits=[(p.sT, mm)], inc=(p.sA, 1))
                ctx.guard[("a1p", qc % 2)] = (p.sA, p.sA.n)
            else:
                vs = ctx.hbig[:, 10240 + (qc % 2) * 512: 10240 + (qc % 2) * 512 + 512]
                vw = [(p.sT, mm)]
                prev_v = ctx.guard.get(("a1v", qc % 2), 0)
                if prev_v:
                    vw.append((p.sX, prev_v))
                p.emit("scalar", lambda e, o=vs, a=ps[:]:
                       e.copy(out=o, in_=a), waits=vw, inc=(p.sA, 1))
                ctx.guard[("a1p", qc % 2)] = (p.sA, p.sA.n)
                st = p.dma("sync", wd["v_dram"][(fc - 16) * 128:(fc - 15) * 128, qc * 512:(qc + 1) * 512],
                           vs, waits=[(p.sA, p.sA.n)])
                ctx.guard[("a1v", qc % 2)] = st
    qk_tok = p.sA.n
    v_all = p.sX.n

    tw = []
    if hb_guard:
        tw.append((p.sT, hb_guard))
    p.dma("sync", hbf[:, 0:2048], wd["cos_tbl"], waits=tw)
    p.dma("sync", hbf[:, 2048:4096], wd["sin_tbl"])
    tbl_tok = p.sX.n

    def q_head(h, sl):
        return ctx.wbig[(h % 2) * 64:(h % 2) * 64 + 64, QF + (h // 2) * 2048 + sl.start: QF + (h // 2) * 2048 + sl.stop]

    def k_head(h, sl):
        return ctx.wbig[(h % 2) * 64:(h % 2) * 64 + 64, KF + (h // 2) * 2048 + sl.start: KF + (h // 2) * 2048 + sl.stop]

    rotscr = ctx.rotscr
    for h in range(NH):
        pp = (h % 2) * 64
        hsl = slice(pp, pp + 64)
        vtm = ctx.hbig[:, 8192 + (h % 2) * 1024: 8192 + (h % 2) * 1024 + 1024]
        vw = [(p.sX, v_all)]
        prev = ctx.guard.get(("a1vt", h % 2))
        if prev:
            vw.append(prev)
        if h < 2:
            pw = ctx.guard.get(("a1w", h % 2))
            if pw:
                vw.append(pw)
        for kc in range(16):
            p.dma("sync", vtm[:, kc * 64:(kc + 1) * 64],
                  wd["v_dram"][h * 64:(h + 1) * 64, kc * 128:(kc + 1) * 128],
                  waits=vw if kc == 0 else (), transpose=True)
        vtm_tok = p.sX.n

        qr = abf[hsl, 0:2048]
        kr = abf[hsl, 2048:4096]
        rot_jobs = [(q_head, qr), (k_head, kr)]
        for ji, (src_head, dstb) in enumerate(rot_jobs):
            for sc in range(4):
                ssl = slice(sc * 512, (sc + 1) * 512)
                pperm = ctx.pb[6]
                mw = [(p.sA, qk_tok), (p.sX, tbl_tok)]
                prev = ctx.guard.get("a1perm")
                if prev:
                    mw.append(prev)
                p.emit("tensor", lambda e, o=pperm[0:64, :], l=ctx.pswap[hsl, :],
                       r=src_head(h, ssl):
                       e.matmul(out=o, lhsT=l, rhs=r, start=True, stop=True),
                       waits=mw, inc=(p.sT, 1))
                pm_tok = p.sT.n
                p.emit("vector", lambda e, o=rotscr[hsl, 0:512], a=src_head(h, ssl),
                       b=hbf[hsl, ssl]:
                       e.tensor_mul(out=o, in0=a, in1=b),
                       waits=[(p.sV, p.sV.n), (p.sX, tbl_tok), (p.sA, qk_tok)],
                       inc=(p.sV, 1))
                p.emit("vector", lambda e, o=rotscr[hsl, 512:1024], a=ctx.pb[6][0:64, :],
                       b=hbf[hsl, 2048 + sc * 512:2048 + (sc + 1) * 512]:
                       e.tensor_mul(out=o, in0=a, in1=b),
                       waits=[(p.sT, pm_tok), (p.sV, p.sV.n)], inc=(p.sV, 1))
                rtw = [(p.sV, p.sV.n)]
                pqr = ctx.guard.get(("a1qr", h % 2))
                if pqr:
                    rtw.append(pqr)
                p.emit("vector", lambda e, o=dstb[:, ssl], a=rotscr[hsl, 0:512],
                       b=rotscr[hsl, 512:1024]:
                       e.tensor_add(out=o, in0=a, in1=b),
                       waits=rtw, inc=(p.sV, 1))
                ctx.guard["a1perm"] = (p.sV, p.sV.n)
        rot_tok = p.sV.n

        den = ctx.sm32[:, 384:400]
        for qg in range(16):
            pd = ctx.pb[qg % 2]
            dumps = ctx.hbig[:, 10240 + (qg % 2) * 512: 10240 + (qg % 2) * 512 + 512]
            parts = ctx.sm32[:, 416 + (qg % 2) * 4: 420 + (qg % 2) * 4]
            for kc in range(4):
                mw = [(p.sA, qk_tok)]
                prev = ctx.guard.get(("a1d", qg % 2))
                if prev:
                    mw.append(prev)
                p.emit("tensor", lambda e, o=pd[:], l=q_head(h, slice(qg * 128, (qg + 1) * 128)),
                       r=k_head(h, slice(kc * 512, (kc + 1) * 512)):
                       e.matmul(out=o, lhsT=l, rhs=r, start=True, stop=True),
                       waits=mw, inc=(p.sT, 1))
                dmm = p.sT.n
                dpw = [(p.sT, dmm)]
                if h == 0 and qg == 0 and kc == 0:
                    dpw.append((p.sX, v_all))
                p.emit("scalar", lambda e, o=dumps, a=pd[:], acc=parts[:, kc:kc + 1]:
                       e.activation(out=o, in_=a, func=AF.Exp, scale=0.125, accum_out=acc),
                       waits=dpw, inc=(p.sA, 1))
                ctx.guard[("a1d", qg % 2)] = (p.sA, p.sA.n)
            de_tok = p.sA.n
            p.emit("vector", lambda e, o=den[:, qg:qg + 1], i=parts:
                   e.reduce_sum(out=o, in_=i, axis=AX),
                   waits=[(p.sA, de_tok), (p.sV, p.sV.n)], inc=(p.sV, 1))
        p.emit("vector", lambda e, o=ctx.sm32[:, 400:416], i=den:
               e.reciprocal(out=o, in_=i), waits=[(p.sV, p.sV.n)], inc=(p.sV, 1))
        rd_tok = p.sV.n
        trw = [(p.sV, rd_tok)]
        prev = ctx.guard.get("a1dtr")
        if prev:
            trw.append(prev)
        p.emit("tensor", lambda e, o=ctx.pb[7][0:16, 0:128], i=ctx.sm32[:, 400:416],
               idt=ctx.ident_f32[:]:
               e.transpose(out=o, in_=i, identity=idt), waits=trw, inc=(p.sT, 1))
        dtr_tok = p.sT.n
        cw2 = [(p.sT, dtr_tok)]
        prev = ctx.guard.get("a1dcp")
        if prev:
            cw2.append(prev)
        p.emit("scalar", lambda e, o=ctx.smbf[0:16, 320:448], a=ctx.pb[7][0:16, 0:128]:
               e.copy(out=o, in_=a), waits=cw2, inc=(p.sA, 1))
        ctx.guard["a1dtr"] = (p.sA, p.sA.n)
        dd = p.dma("sync", wd["den_dram"][h].rearrange("(g q) -> g q", q=128),
                   ctx.smbf[0:16, 320:448], waits=[(p.sA, p.sA.n)])
        ctx.guard["a1dcp"] = (p.sX, dd)

        for qc in range(4):
            qsl = slice(qc * 512, (qc + 1) * 512)
            ld = p.dma("sync", ctx.rdbf[0:1, :], wd["den_dram"][h:h + 1, qsl],
                       waits=[(p.sX, dd), (p.sT, ctx.guard.get("a1rbm", (None, 0))[1])])
            prev = ctx.guard.get("a1rb")
            rw = [(p.sX, ld)]
            if prev:
                rw.append(prev)
            p.emit("tensor", lambda e, o=ctx.pb[5][:], l=ctx.smbf[0:1, 0:128], r=ctx.rdbf[0:1, :]:
                   e.matmul(out=o, lhsT=l, rhs=r, start=True, stop=True),
                   waits=rw, inc=(p.sT, 1))
            rbm_tok = p.sT.n
            ctx.guard["a1rbm"] = (p.sT, rbm_tok)
            p.emit("scalar", lambda e, o=ctx.smbf[:, 512:1024], a=ctx.pb[5][:]:
                   e.copy(out=o, in_=a), waits=[(p.sT, rbm_tok)], inc=(p.sA, 1))
            rdbc_tok = p.sA.n
            ctx.guard["a1rb"] = (p.sA, rdbc_tok)

            po = ctx.pb[4]
            nkc = qc * 4 + 4
            for kc in range(nkc):
                pat = ctx.pb[kc % 2]
                pex = ctx.pb[2 + kc % 2]
                mw = [(p.sV, rot_tok)]
                prev = ctx.guard.get(("a1n", kc % 2))
                if prev:
                    mw.append(prev)
                p.emit("tensor", lambda e, o=pat[:], l=kr[:, kc * 128:(kc + 1) * 128],
                       r=qr[:, qsl]:
                       e.matmul(out=o, lhsT=l, rhs=r, start=True, stop=True),
                       waits=mw, inc=(p.sT, 1))
                at_tok = p.sT.n
                ew = [(p.sT, at_tok)]
                prev = ctx.guard.get(("a1e", kc % 2))
                if prev:
                    ew.append(prev)
                p.emit("scalar", lambda e, o=pex[:], a=pat[:]:
                       e.activation(out=o, in_=a, func=AF.Exp, scale=0.125),
                       waits=ew, inc=(p.sA, 1))
                ex_tok = p.sA.n
                ctx.guard[("a1n", kc % 2)] = (p.sA, ex_tok)
                pt = ctx.hbig[:, 10240 + (kc % 2) * 512: 10240 + (kc % 2) * 512 + 512]
                ptw = [(p.sA, ex_tok), (p.sA, rdbc_tok), (p.sV, p.sV.n)]
                prev = ctx.guard.get(("a1pt", kc % 2))
                if prev:
                    ptw.append(prev)
                diag = (kc >= qc * 4)
                if diag:
                    off = 384 - (kc * 128 - qc * 512)
                    p.emit("vector", lambda e, o=pt, a=pex[:], b=ctx.smbf[:, 512:1024]:
                           e.tensor_mul(out=o, in0=a, in1=b), waits=ptw, inc=(p.sV, 1))
                    p.emit("vector", lambda e, o=pt, a=pt, b=ctx.trib[:, off:off + 512]:
                           e.tensor_mul(out=o, in0=a, in1=b),
                           waits=[(p.sV, p.sV.n)], inc=(p.sV, 1))
                else:
                    p.emit("vector", lambda e, o=pt, a=pex[:], b=ctx.smbf[:, 512:1024]:
                           e.tensor_mul(out=o, in0=a, in1=b), waits=ptw, inc=(p.sV, 1))
                pt_tok = p.sV.n
                ctx.guard[("a1e", kc % 2)] = (p.sV, pt_tok)
                vw2 = [(p.sV, pt_tok), (p.sX, vtm_tok)]
                if kc == 0:
                    prev = ctx.guard.get("a1o")
                    if prev:
                        vw2.append(prev)
                p.emit("tensor", lambda e, o=po[0:64, :], l=vtm[:, kc * 64:(kc + 1) * 64],
                       r=pt, st=(kc == 0), sp=(kc == nkc - 1):
                       e.matmul(out=o, lhsT=l, rhs=r, start=st, stop=sp),
                       waits=vw2, inc=(p.sT, 1))
                ctx.guard[("a1pt", kc % 2)] = (p.sT, p.sT.n)
            o_tok = p.sT.n
            dc = h // 2
            ysl = slice(dc * 2048 + qc * 512, dc * 2048 + (qc + 1) * 512)
            if pp == 0:
                yw = [(p.sT, o_tok)]
                prev = ctx.guard.get("a1y")
                if prev:
                    yw.append(prev)
                p.emit("scalar", lambda e, o=ctx.xnbig[0:64, ysl], a=po[0:64, :]:
                       e.copy(out=o, in_=a), waits=yw, inc=(p.sA, 1))
                ctx.guard["a1y"] = (p.sA, p.sA.n)
                ctx.guard["a1o"] = (p.sA, p.sA.n)
            else:
                p.emit("scalar", lambda e, o=ctx.smbf[0:64, 512:1024], a=po[0:64, :]:
                       e.copy(out=o, in_=a),
                       waits=[(p.sT, o_tok), (p.sA, rdbc_tok)], inc=(p.sA, 1))
                ybf_tok = p.sA.n
                ctx.guard["a1rb"] = (p.sA, ybf_tok)
                prev = ctx.guard.get("a1rl")
                rl = [(p.sA, ybf_tok)]
                if prev:
                    rl.append(prev)
                p.emit("tensor", lambda e, o=ctx.pb[6][64:128, :], l=ctx.smbf[0:64, 128:192],
                       r=ctx.smbf[0:64, 512:1024]:
                       e.matmul(out=o, lhsT=l, rhs=r, start=True, stop=True,
                                tile_position=(0, 64)),
                       waits=rl, inc=(p.sT, 1))
                rl_tok = p.sT.n
                p.emit("scalar", lambda e, o=ctx.xnbig[64:128, ysl], a=ctx.pb[6][64:128, :]:
                       e.copy(out=o, in_=a), waits=[(p.sT, rl_tok)], inc=(p.sA, 1))
                ctx.guard["a1rl"] = (p.sA, p.sA.n)
                ctx.guard["a1o"] = (p.sA, p.sA.n)
                ctx.guard["a1perm"] = (p.sA, p.sA.n)
        ctx.guard[("a1vt", h % 2)] = (p.sT, p.sT.n)
        ctx.guard[("a1qr", h % 2)] = (p.sT, p.sT.n)
    y_tok = p.sA.n

    ow = []
    prev = ctx.guard.get("a1qkfree")
    p.dma("gpsimd", ctx.wbig[:, 0:8192].rearrange("p (dc f) -> p dc f", f=1024),
          wd["w_outT"].rearrange("(dc p) f -> p dc f", p=128),
          waits=[(p.sT, p.sT.n)])
    wo_tok = p.sW.n
    p.dma("sync", ctx.sm32[:, 16:24], wd["bout_sb"])
    bo_tok = p.sX.n
    for fc in range(8):
        for qc in range(4):
            ps = ctx.pb[qc % 2]
            w0 = [(p.sW, wo_tok), (p.sA, y_tok)]
            prev = ctx.guard.get(("a1op", qc % 2))
            if prev:
                w0.append(prev)
            for dc in range(DC):
                p.emit("tensor", lambda e, o=ps[:],
                       l=ctx.wbig[:, dc * 1024 + fc * 128: dc * 1024 + fc * 128 + 128],
                       r=ctx.xnbig[:, dc * 2048 + qc * 512: dc * 2048 + (qc + 1) * 512],
                       st=(dc == 0), sp=(dc == DC - 1):
                       e.matmul(out=o, lhsT=l, rhs=r, start=st, stop=sp),
                       waits=w0 if dc == 0 else [],
                       inc=(p.sT, 1) if dc == DC - 1 else None)
            mm = p.sT.n
            p.emit("vector", lambda e, o=ctx.x(fc)[:, qc * 512:(qc + 1) * 512],
                   a=ps[:], b=ctx.sm32[:, 16 + fc:17 + fc],
                   c=ctx.x(fc)[:, qc * 512:(qc + 1) * 512]:
                   e.scalar_tensor_tensor(out=o, in0=a, scalar=b, in1=c,
                                          op0=ALU.add, op1=ALU.add),
                   waits=[(p.sT, mm), (p.sX, bo_tok), (p.sV, p.sV.n)], inc=(p.sV, 1))
            ctx.guard[("a1op", qc % 2)] = (p.sV, p.sV.n)
            ctx.x_tok[fc] = p.sV.n
    ctx.guard["hbig_free"] = p.sT.n
    ctx.guard["xn_free"] = p.sT.n
    ctx.guard["gu_done"] = 0
    ctx.guard["down_done"] = p.sT.n


def emit_attn1_sublayer(ctx, wd):
    emit_ln(ctx, 0, full=True)
    t1 = emit_ln(ctx, 1, full=True)
    ctx._attn_xn_tok = t1
    emit_attn1(ctx, wd)


def declare_attn1_weights(nc):
    wd = {}
    wd["w_qkvT"] = nc.dram_tensor("w_qkvT", [D, 3 * D], F32, kind="ExternalInput").ap()
    wd["w_outT"] = nc.dram_tensor("w_outT", [D, D], F32, kind="ExternalInput").ap()
    wd["bout_sb"] = nc.dram_tensor("bout_sb", [128, 8], F32, kind="ExternalInput").ap()
    wd["cos_tbl"] = nc.dram_tensor("cos_tbl", [128, T], F32, kind="ExternalInput").ap()
    wd["sin_tbl"] = nc.dram_tensor("sin_tbl", [128, T], F32, kind="ExternalInput").ap()
    wd["pswap_t"] = nc.dram_tensor("pswap_t", [128, 64], F32, kind="ExternalInput").ap()
    wd["trib_t"] = nc.dram_tensor("trib_t", [128, 896], F32, kind="ExternalInput").ap()
    wd["v_dram"] = nc.dram_tensor("v_dram", [D, T], BF16, kind="Internal").ap()
    wd["den_dram"] = nc.dram_tensor("den_dram", [NH, T], BF16, kind="Internal").ap()
    return wd


def emit_attn1_consts(ctx, wd):
    p = ctx.p
    p.dma("gpsimd", ctx.pswap[:], wd["pswap_t"])
    p.dma("gpsimd", ctx.trib[:], wd["trib_t"])
    ctx.guard["a1c"] = p.sW.n


def build_nc_attn1():
    nc = bass.Bass("TRN2", target_bir_lowering=False, detect_race_conditions=False)
    p = Prog(nc)
    xin = nc.dram_tensor("xin", [D, T], F32, kind="ExternalInput").ap()
    xout = nc.dram_tensor("xout", [D, T], F32, kind="ExternalOutput").ap()
    wd = declare_attn1_weights(nc)
    ctx = Ctx(nc, p)
    ctx.rotscr = ctx.arena[:, 2048:3072]
    ctx.rdbf = nc.alloc_sbuf_tensor("rdbf", [1, 512], BF16)
    ctx.pswap = nc.alloc_sbuf_tensor("pswap", [128, 64], BF16)
    ctx.trib = nc.alloc_sbuf_tensor("trib", [128, 896], BF16)
    emit_prologue(ctx, xin, need_moe=False)
    emit_attn1_consts(ctx, wd)
    emit_attn1_sublayer(ctx, wd)
    emit_epilogue(ctx, xout)
    p.build()
    return nc


def _attn1_consts():
    i = np.arange(32, dtype=np.float64)
    theta = 10000.0 ** (-2.0 * (i - 1.0) / 64.0)
    t = np.arange(T, dtype=np.float64)
    ang = t[None, :] * theta[:, None]
    cos32 = np.cos(ang)
    sin32 = np.sin(ang)
    cos64 = np.repeat(cos32, 2, axis=0)
    sin64 = np.repeat(sin32, 2, axis=0)
    sin64[1::2] *= -1.0
    cos128 = np.tile(cos64, (2, 1)).astype(np.float32)
    sin128 = np.tile(sin64, (2, 1)).astype(np.float32)
    P = np.zeros((64, 64), np.float32)
    for m in range(32):
        P[2 * m + 1, 2 * m] = 1.0
        P[2 * m, 2 * m + 1] = 1.0
    pswap = np.tile(P, (2, 1)).astype(np.float32)
    trib = np.zeros((128, 896), np.float32)
    for ii in range(128):
        for jj in range(896):
            if (jj - 384) >= ii:
                trib[ii, jj] = 1.0
    return cos128, sin128, pswap, trib


def attn1_in_maps(x_full, inputs):
    ident, _ = _host_constants()
    cos128, sin128, pswap, trib = _attn1_consts()
    bout = np.asarray(inputs["L1_attn_b_out"], np.float32).reshape(8, 128).T.copy()
    wqkvT = np.ascontiguousarray(np.asarray(inputs["L1_attn_w_qkv"], np.float32).T)
    woutT = np.ascontiguousarray(np.asarray(inputs["L1_attn_w_out"], np.float32).T)
    maps = []
    for b in range(NCORE):
        maps.append({
            "xin": np.ascontiguousarray(np.asarray(x_full[b], np.float32).T),
            "ident": ident,
            "w_qkvT": wqkvT,
            "w_outT": woutT,
            "bout_sb": np.ascontiguousarray(bout),
            "cos_tbl": cos128,
            "sin_tbl": sin128,
            "pswap_t": pswap,
            "trib_t": trib,
        })
    return maps


def run_attn1_phase(x_full, inputs):
    nc = build_nc_attn1()
    res = run_bass_kernel_spmd(nc, attn1_in_maps(x_full, inputs), core_ids=list(range(NCORE)))
    return np.stack([np.ascontiguousarray(r["xout"].T) for r in res.results])


ALL_PHASES = ("moe", "attn0", "moe", "ffn1", "attn1", "ffn1")


def build_nc_span(phases):
    nc = bass.Bass("TRN2", target_bir_lowering=False, detect_race_conditions=False)
    p = Prog(nc)
    xin = nc.dram_tensor("xin", [D, T], F32, kind="ExternalInput").ap()
    xout = nc.dram_tensor("xout", [D, T], F32, kind="ExternalOutput").ap()
    need_moe = "moe" in phases
    n_ffn = sum(1 for ph in phases if ph.startswith("ffn1"))
    wd = {}
    if need_moe:
        wd.update(declare_moe_weights(nc))
    if "attn0" in phases:
        wd.update(declare_attn0_weights(nc))
    if n_ffn:
        wd.update(declare_ffn1_weights(nc, with_prep=True))
        if n_ffn > 1:
            wd["hq_dram_b"] = nc.dram_tensor("hq_dram_b", [T, F_L1], BF16, kind="Internal").ap()
    if "attn1" in phases:
        wd.update(declare_attn1_weights(nc))
    ctx = Ctx(nc, p)
    if "attn1" in phases:
        ctx.rotscr = ctx.arena[:, 2048:3072]
        ctx.rdbf = nc.alloc_sbuf_tensor("rdbf", [1, 512], BF16)
        ctx.pswap = nc.alloc_sbuf_tensor("pswap", [128, 64], BF16)
        ctx.trib = nc.alloc_sbuf_tensor("trib", [128, 896], BF16)
    emit_prologue(ctx, xin, need_moe=need_moe)
    if "attn1" in phases:
        emit_attn1_consts(ctx, wd)
    ffn_i = 0
    seen_attn1 = False
    for ph in phases:
        dd = ctx.guard.get("down_done", 0)
        if ph == "moe":
            ctx.guard[("gu_done", 0)] = dd
            ctx.guard[("gu_done", 1)] = dd
            emit_moe_sublayer(ctx, wd)
        elif ph == "attn0":
            emit_attn0_sublayer(ctx, wd)
        elif ph.startswith("ffn1"):
            stage = 99 if ph == "ffn1" else int(ph[5:])
            if ffn_i == 0:
                if dd:
                    ctx.guard[("wp", 0)] = (p.sT, dd)
                    ctx.guard[("wp", 1)] = (p.sT, dd)
                emit_wprep(ctx, wd)
            if dd:
                ctx.guard[("f1g", 0)] = (p.sT, dd)
                ctx.guard[("f1g", 1)] = (p.sT, dd)
                ctx.guard["wdq_free"] = (p.sT, dd)
            wdx = dict(wd)
            if ffn_i == 1:
                wdx["hq_dram"] = wd["hq_dram_b"]
            emit_ffn1(ctx, wdx, "f1_%d" % ffn_i, stage=stage,
                      banks=(4, 4, 2) if seen_attn1 else (0, 2, 6))
            ffn_i += 1
        elif ph == "attn1":
            emit_attn1_sublayer(ctx, wd)
            seen_attn1 = True
    emit_epilogue(ctx, xout)
    p.build()
    return nc


def span_in_maps(x_full, inputs, phases):
    ident, onehot = _host_constants()
    common = {"ident": ident}
    if "moe" in phases:
        common.update({
            "gate_w_sb": _rearrange_gate_w(np.asarray(inputs["L0_ffn_gate_w"], np.float32)),
            "onehot_bf": onehot,
            "exp_gate": np.asarray(inputs["L0_ffn_exp_gate"], np.float32),
            "exp_up": np.asarray(inputs["L0_ffn_exp_up"], np.float32),
            "exp_down": np.asarray(inputs["L0_ffn_exp_down"], np.float32),
            "sh_gate": np.asarray(inputs["L0_ffn_sh_gate"], np.float32),
            "sh_up": np.asarray(inputs["L0_ffn_sh_up"], np.float32),
            "sh_down": np.asarray(inputs["L0_ffn_sh_down"], np.float32),
        })
    if "attn0" in phases:
        common.update({
            "Wq": np.asarray(inputs["L0_attn_Wq"], np.float32),
            "Wk": np.asarray(inputs["L0_attn_Wk"], np.float32),
            "bq_sb": np.ascontiguousarray(np.asarray(inputs["L0_attn_bq"], np.float32).reshape(8, 128).T),
            "bk_sb": np.ascontiguousarray(np.asarray(inputs["L0_attn_bk"], np.float32).reshape(8, 128).T),
        })
    if any(ph.startswith("ffn1") for ph in phases):
        common.update({
            "w_gate": np.asarray(inputs["L1_ffn_w_gate"], np.float32),
            "w_up": np.asarray(inputs["L1_ffn_w_up"], np.float32),
            "w_down": np.asarray(inputs["L1_ffn_w_down"], np.float32),
        })
    if "attn1" in phases:
        cos128, sin128, pswap, trib = _attn1_consts()
        common.update({
            "w_qkvT": np.ascontiguousarray(np.asarray(inputs["L1_attn_w_qkv"], np.float32).T),
            "w_outT": np.ascontiguousarray(np.asarray(inputs["L1_attn_w_out"], np.float32).T),
            "bout_sb": np.ascontiguousarray(np.asarray(inputs["L1_attn_b_out"], np.float32).reshape(8, 128).T),
            "cos_tbl": cos128,
            "sin_tbl": sin128,
            "pswap_t": pswap,
            "trib_t": trib,
        })
    maps = []
    for b in range(NCORE):
        m = dict(common)
        m["xin"] = np.ascontiguousarray(np.asarray(x_full[b], np.float32).T)
        maps.append(m)
    return maps


def run_span(x_full, inputs, phases):
    nc = build_nc_span(phases)
    res = run_bass_kernel_spmd(nc, span_in_maps(x_full, inputs, phases),
                               core_ids=list(range(NCORE)))
    return np.stack([np.ascontiguousarray(r["xout"].T) for r in res.results])


def kernel(**inputs):
    x = np.asarray(inputs["L0_attn_x"], np.float32)
    return run_span(x, inputs, ALL_PHASES)
